# version 16
# baseline (speedup 1.0000x reference)
import numpy as np
import concourse.bass as bass
import concourse.mybir as mybir
from concourse.bass_utils import run_bass_kernel_spmd

F32 = mybir.dt.float32
BF16 = mybir.dt.bfloat16
AF = mybir.ActivationFunctionType
ALU = mybir.AluOpType
AXX = mybir.AxisListType.X

B2 = 2
N = 4096
T = B2 * N
D = 1024
E = 8
SH = 256
KCAP = 2048
MTT = 1024
NMT = T // MTT
NT2 = MTT // 512
BIS_ITERS = 30

ENGS = ("sync", "scalar", "vector", "tensor", "gpsimd")


class Prog:
    def __init__(self, nc):
        self.nc = nc
        self.ops = {k: [] for k in ENGS}
        self.cnt = {}
        self.watermark = {k: {} for k in ENGS}

    def sem(self, name):
        h = self.nc.alloc_semaphore(name)
        self.cnt[h] = 0
        return h

    def dma(self, q, out, in_, sem, **kw):
        self.cnt[sem] += 16
        tgt = self.cnt[sem]

        def fn(e, out=out, in_=in_, kw=kw, sem=sem):
            e.dma_start(out=out, in_=in_, **kw).then_inc(sem, 16)

        self.ops[q].append(fn)
        return (sem, tgt)

    def op(self, eng, build, inc=None):
        if inc is not None:
            self.cnt[inc] += 1
            tgt = self.cnt[inc]

            def fn(e, build=build, inc=inc):
                build(e).then_inc(inc, 1)

            self.ops[eng].append(fn)
            return (inc, tgt)
        self.ops[eng].append(lambda e, build=build: build(e))
        return None

    def wait(self, eng, tok):
        if tok is None:
            return
        sem, tgt = tok
        if tgt <= self.watermark[eng].get(sem, 0):
            return
        self.watermark[eng][sem] = tgt
        self.ops[eng].append(lambda e, sem=sem, tgt=tgt: e.wait_ge(sem, tgt))

    def mark(self, sem):
        return (sem, self.cnt[sem])

    def vsync(self, s_vec):
        tok = self.op("vector", lambda e: e.engine_nop(), inc=s_vec)
        self.wait("vector", tok)

    def barrier(self, sems):
        toks = [self.mark(s) for s in sems]
        for eng in ENGS:
            for t in toks:
                self.wait(eng, t)


def build_nc(dbg=(), stage=5):
    nc = bass.Bass("TRN2", target_bir_lowering=False, detect_race_conditions=False)
    p = Prog(nc)
    W = p.wait
    M = p.mark

    dt_ = nc.dram_tensor
    x_in = dt_("x", [T, D], F32, kind="ExternalInput").ap()
    wqkv_in = dt_("w_qkv", [3 * D, D], F32, kind="ExternalInput").ap()
    wout_in = dt_("w_out", [D, D], F32, kind="ExternalInput").ap()
    bout_in = dt_("b_out", [1, D], F32, kind="ExternalInput").ap()
    rf_in = dt_("rf", [64, 64], F32, kind="ExternalInput").ap()
    wg_in = dt_("w_gate", [D, E], F32, kind="ExternalInput").ap()
    wgb_in = dt_("w_gate_b", [1, E], F32, kind="ExternalInput").ap()
    w1_in = dt_("W1", [E * D, SH], F32, kind="ExternalInput").ap()
    b1_in = dt_("b1", [E, SH], F32, kind="ExternalInput").ap()
    w2_in = dt_("W2", [E * SH, D], F32, kind="ExternalInput").ap()
    b2_in = dt_("b2", [E, D], F32, kind="ExternalInput").ap()
    ident_in = dt_("ident128", [128, 128], F32, kind="ExternalInput").ap()
    ones_in = dt_("onesM", [128, 512], F32, kind="ExternalInput").ap()
    emat_in = dt_("Emat", [16, D], F32, kind="ExternalInput").ap()
    etmat_in = dt_("Etmat", [D, 16], F32, kind="ExternalInput").ap()
    rowsel_in = dt_("Rowsel", [E, E * 128], F32, kind="ExternalInput").ap()

    y_out = dt_("y", [T, D], F32, kind="ExternalOutput").ap()

    def scratch(name, shape, dtype):
        kind = "ExternalOutput" if name in dbg else "Internal"
        return dt_(name, shape, dtype, kind=kind).ap()

    xb_h = scratch("xb_h", [T, D], BF16)
    wqkv_bh = scratch("wqkv_bh", [3 * D, D], BF16)
    wout_bh = scratch("wout_bh", [D, D], BF16)
    qp_h = scratch("qp_h", [D, T], BF16)
    kp_h = scratch("kp_h", [D, T], BF16)
    v_h = scratch("v_h", [D, T], BF16)
    xmid32_h = scratch("xmid32_h", [T, D], F32)
    xmidb_h = scratch("xmidb_h", [T, D], BF16)
    cc_in = dt_("cc_in", [128, 256], F32, kind="Internal").ap()
    cc_out = dt_("cc_out", [128, 256], F32, kind="Internal", addr_space="Shared").ap()
    route_d = scratch("route_dbg", [16, N], F32) if "route_dbg" in dbg else None
    sel_d = scratch("sel_dbg", [16, N], BF16) if "sel_dbg" in dbg else None
    hg_d = scratch("hg_dbg", [128, 16384], BF16) if "hg_dbg" in dbg else None
    sbc_d = scratch("sbc_dbg", [128, 8192], BF16) if "sbc_dbg" in dbg else None
    y1_d = scratch("y1_dbg", [MTT, D], F32) if "y1_dbg" in dbg else None
    ps_d = scratch("ps_dbg", [MTT, D], F32) if "ps_dbg" in dbg else None
    cy1_d = scratch("cy1_dbg", [512, D], F32) if "cy1_dbg" in dbg else None
    cst_d = scratch("cst_dbg", [512, 16], F32) if "cst_dbg" in dbg else None
    cxr_d = scratch("cxr_dbg", [512, D], F32) if "cxr_dbg" in dbg else None

    sb = nc.alloc_sbuf_tensor
    AR_W = sb("AR_W", [128, 32768], BF16)
    wqkvT = AR_W[:, 0:24576]
    w_outT = AR_W[:, 24576:32768]
    W1s = AR_W[:, 0:16384]
    W2s = AR_W[:, 16384:32768]
    AR_ACT = sb("AR_ACT", [128, 16384], BF16)
    q_bf = AR_ACT[:, 0:8192]
    k_bf = AR_ACT[:, 8192:16384]
    hgm = AR_ACT
    AR_X = sb("AR_X", [128, 8192], BF16)
    AR_C = sb("AR_C", [128, 16384], BF16)
    qtile = AR_C[:, 0:8192]
    outsc = AR_C[:, 8192:16384]
    kvload = AR_C[:, 0:4096]
    q2buf = AR_C[:, 4096:8192]
    vstage = AR_C[:, 8192:9216]
    junk_bf = AR_C[0:16, 0:4096]
    cand_a = AR_C[0:16, 4096:8192]
    cand_b = AR_C[0:16, 8192:12288]
    selhi = AR_C[0:16, 12288:16384]
    selpm_t = AR_C[0:16, 0:4096]
    selbc = AR_C[:, 0:8192]
    F32A = sb("F32A", [128, 6144], F32)
    normf = F32A[0:16, 0:1024]
    b1stage = F32A[0:8, 2048:2304]
    xres = F32A[:, 0:2048]
    y1 = F32A[:, 2048:3072]
    sqjunk = F32A[:, 3072:4096]
    xmid_sb = F32A[:, 4096:6144]
    route_pm = F32A[0:16, 0:4096]
    tier = F32A[0:16, 0:4096]
    den_sb = F32A[:, 4096:4352]
    part_sb = F32A[:, 4352:4608]
    route_tok = F32A[:, 4608:4640]
    lg_sb = F32A[0:8, 5120:5632]
    masked_all = F32A[:, 5632:6144]
    hpad = F32A[:, 0:2048]
    yout_sb = F32A[:, 4096:6144]

    xmidb_sb = AR_ACT[:, 0:2048]
    sbc_sb = AR_ACT[:, 2048:3072]
    s_bf = AR_ACT[0:16, 3072:3584]
    identb = AR_ACT[:, 3584:3712]
    sel_b0 = AR_C[0:8, 8192:12288]
    sel_b1 = AR_C[0:8, 12288:16384]

    ident_f = sb("ident_f", [128, 128], F32)
    ones_b = sb("ones_b", [128, 512], BF16)
    rowsel = sb("rowsel", [8, 1024], BF16)
    rfbd = sb("rfbd", [128, 128], BF16)
    E_sb = sb("E_sb", [16, 1024], BF16)
    Et_sb = sb("Et_sb", [128, 128], BF16)
    wg_b = sb("wg_b", [128, 64], BF16)
    wgb_b = sb("wgb_b", [1, 8], BF16)
    bout_b = sb("bout_b", [1, 1024], BF16)
    b1s = sb("b1s", [128, 16], F32)
    b2s = sb("b2s", [8, 1024], BF16)
    normn = sb("normn", [16, 1024], BF16)
    kvstage = sb("kvstage", [128, 2048], BF16)
    kswide = sb("kswide", [128, 256], BF16)
    nh_sb = sb("nh_sb", [16, 512], F32)
    stats = sb("stats", [128, 16], F32)
    exp_sb = sb("exp_sb", [128, 16], F32)
    gss = sb("gss", [128, 48], F32)
    bis = sb("bis", [16, 12], F32)
    eps_sb = sb("eps_sb", [128, 1], F32)

    pbank = [nc.alloc_psum_tensor(f"pb{i}", [128, 512], F32) for i in range(8)]

    s_ld = p.sem("s_ld")
    s_st = p.sem("s_st")
    s_gp = p.sem("s_gp")
    s_pe = p.sem("s_pe")
    s_vec = p.sem("s_vec")
    s_scl = p.sem("s_scl")
    s_cc = p.sem("s_cc")
    s_out = p.sem("s_out")

    p.dma("gpsimd", ones_b[:], ones_in[:], s_gp)
    p.dma("gpsimd", E_sb[:], emat_in[:], s_gp)
    p.dma("gpsimd", rowsel[:], rowsel_in[:], s_gp)
    for c in range(8):
        p.dma("gpsimd", Et_sb[:, 16 * c:16 * c + 16],
              etmat_in[128 * c:128 * c + 128, :], s_gp)
    for dc in range(8):
        p.dma("gpsimd", wg_b[:, 8 * dc:8 * dc + 8],
              wg_in[128 * dc:128 * dc + 128, :], s_gp)
    p.dma("gpsimd", wgb_b[:], wgb_in[:], s_gp)
    p.dma("gpsimd", bout_b[:], bout_in[:], s_gp)
    p.dma("gpsimd", b2s[:], b2_in[:], s_gp)
    t_consts = p.mark(s_gp)
    t_ident = p.dma("sync", ident_f[:], ident_in[:], s_ld)
    t_b1 = p.dma("sync", b1stage[:], b1_in[:], s_ld)
    rf_f = F32A[0:64, 1024:1088]
    rf_t = p.dma("sync", rf_f, rf_in[:], s_ld)
    t_wload = p.mark(s_ld)

    t_xb = [None]
    for mt in range(1, NMT):
        r0 = mt * MTT
        t_xb.append(p.dma("gpsimd", xb_h[r0:r0 + MTT, :], x_in[r0:r0 + MTT, :], s_gp))

    W("vector", t_ident)
    p.op("vector", lambda e: e.tensor_copy(identb[:], ident_f[:]))
    p.op("vector", lambda e: e.memset(rfbd[:], 0.0))
    p.op("vector", lambda e: e.memset(kvstage[:], 0.0))
    p.op("vector", lambda e: e.memset(eps_sb[:], 1e-5))
    v_id = p.op("vector", lambda e: e.memset(kswide[:], 0.0), inc=s_vec)
    W("tensor", rf_t)
    W("tensor", v_id)
    pe_rft = p.op("tensor", lambda e: e.transpose(
        pbank[0][0:64, 0:64], rf_f, ident_f[0:64, 0:64]), inc=s_pe)
    W("vector", pe_rft)
    v_rf = p.op("vector", lambda e: e.tensor_scalar_mul(
        rfbd[0:64, 0:64], pbank[0][0:64, 0:64], 0.125), inc=s_vec)
    W("sync", v_rf)
    W("sync", t_wload)
    rf_cp = p.dma("sync", rfbd[64:128, 64:128], rfbd[0:64, 0:64], s_ld)
    W("tensor", t_b1)
    W("tensor", v_rf)
    W("tensor", rf_cp)
    for h in range(2):
        pe_b1 = p.op("tensor", lambda e, h=h: e.transpose(
            pbank[1][:, 0:8], b1stage[:, 128 * h:128 * h + 128],
            ident_f[0:8, 0:8]), inc=s_pe)
        W("vector", pe_b1)
        vb = p.op("vector", lambda e, h=h: e.tensor_copy(
            b1s[:, 8 * h:8 * h + 8], pbank[1][:, 0:8]), inc=s_vec)
        W("tensor", vb)

    stgs = [F32A[:, 4096:5120], F32A[:, 5120:6144]]
    chunks = ([("wq", j) for j in range(24)] + [("x0", j) for j in range(8)]
              + [("wo", j) for j in range(8)])
    evg = {}
    t_wq_pe = t_x0_pe = None
    for ci, (kind, j) in enumerate(chunks):
        stg = stgs[ci % 2]
        eng = "sync" if ci % 2 == 0 else "scalar"
        sem = s_ld if ci % 2 == 0 else s_st
        if ci >= 2:
            W(eng, evg[(ci - 2) * 8 + 7])
        src = {"wq": wqkv_in, "x0": x_in, "wo": wout_in}[kind]
        tl = p.dma(eng, stg[:], src[128 * j:128 * j + 128, :], sem)
        for k in range(8):
            gti = ci * 8 + k
            bi = 2 + gti % 4
            W("tensor", tl)
            if gti >= 4:
                W("tensor", evg[gti - 4])
            pe_f = p.op("tensor", lambda e, bi=bi, stg=stg, k=k: e.transpose(
                pbank[bi][:, 0:128], stg[:, 128 * k:128 * k + 128],
                ident_f[:]), inc=s_pe)
            if kind == "wq":
                dst = wqkvT[:, 3072 * k + 128 * j:3072 * k + 128 * j + 128]
            elif kind == "x0":
                dst = AR_X[:, 1024 * k + 128 * j:1024 * k + 128 * j + 128]
            else:
                dst = w_outT[:, 1024 * k + 128 * j:1024 * k + 128 * j + 128]
            W("scalar", pe_f)
            evg[gti] = p.op("scalar", lambda e, dst=dst, bi=bi: e.copy(
                dst, pbank[bi][:, 0:128]), inc=s_scl)
        if kind == "wq" and j == 23:
            t_wq_pe = p.mark(s_scl)
        if kind == "x0" and j == 7:
            t_x0_pe = p.mark(s_scl)
    t_wo_pe = p.mark(s_scl)

    mk = {}
    if stage <= 0:
        return _cut(nc, p, [s_ld, s_st, s_gp])

    NMT_RUN = NMT if stage > 0.6 else 1
    for mt in range(NMT_RUN):
        b = mt // 4
        tb = mt * MTT
        W("sync", rf_cp)
        if mt > 0:
            W("sync", t_xb[mt])
            W("sync", mk[("yevpe", mt - 1, 24 * NT2 - 1)])
            for c in range(8):
                p.dma("sync", AR_X[:, 1024 * c:1024 * (c + 1)],
                      xb_h[tb:tb + MTT, 128 * c:128 * c + 128], s_ld, transpose=True)
            t_xfm = p.mark(s_ld)
        else:
            t_xfm = t_x0_pe
        if stage <= 0.3:
            mk[("axfm", mt)] = p.mark(s_pe)
            continue

        W("tensor", t_wq_pe)
        W("tensor", t_wo_pe)
        W("tensor", t_consts)
        W("tensor", t_xfm)
        for oc in range(24):
            for nt in range(NT2):
                gi = oc * NT2 + nt
                par = pbank[gi % 2]
                key = ("yev", mt, gi - 2)
                if key in mk:
                    W("tensor", mk[key])
                elif mt > 0:
                    W("tensor", mk[("yev", mt - 1, 24 * NT2 - 2 + gi)])
                for dc in range(8):
                    pe_t = p.op("tensor", lambda e, par=par, dc=dc, oc=oc, nt=nt: e.matmul(
                        par[:],
                        wqkvT[:, 3072 * dc + 128 * oc:3072 * dc + 128 * oc + 128],
                        AR_X[:, 1024 * dc + 512 * nt:1024 * dc + 512 * nt + 512],
                        start=(dc == 0), stop=(dc == 7)),
                        inc=s_pe if dc == 7 else None)
                mk[("yevpe", mt, gi)] = pe_t
                if stage <= 0.35:
                    mk[("yev", mt, gi)] = pe_t
                    continue
                if oc < 16 or stage <= 0.4:
                    buf = q_bf if oc < 8 else k_bf
                    cc = oc % 8
                    if oc >= 16:
                        buf = q_bf
                    dst = buf[:, 1024 * cc + 512 * nt:1024 * cc + 512 * nt + 512]
                    W("scalar", pe_t)
                    ev = p.op("scalar", lambda e, dst=dst, par=par: e.copy(dst, par[:]),
                              inc=s_scl)
                else:
                    vi = ((oc - 16) * NT2 + nt) % 2
                    dst = vstage[:, 512 * vi:512 * vi + 512]
                    key2 = ("vstore", mt, (oc - 16) * NT2 + nt - 2)
                    if key2 in mk:
                        W("vector", mk[key2])
                    W("vector", pe_t)
                    ev = p.op("vector", lambda e, dst=dst, par=par: e.tensor_copy(
                        dst, par[:]), inc=s_vec)
                    W("scalar", ev)
                    st = p.dma("scalar",
                               v_h[128 * (oc - 16):128 * (oc - 16) + 128,
                                   tb + 512 * nt:tb + 512 * nt + 512], dst, s_st)
                    mk[("vstore", mt, (oc - 16) * NT2 + nt)] = st
                mk[("yev", mt, gi)] = ev
        qk_evac = p.mark(s_scl)
        if stage <= 0.7:
            mk[("axfm", mt)] = p.mark(s_pe)
            continue

        for side, buf, hbm in (("k", k_bf, kp_h), ("q", q_bf, qp_h)):
            prevside = ("q", mt - 1) if side == "k" else ("k", mt)
            for nt in range(NT2):
                W("scalar", qk_evac)
                pk = ("ssqpe", side, mt, nt - 1) if nt > 0 else \
                    ("ssqpe", prevside[0], prevside[1], NT2 - 1)
                if pk in mk:
                    W("scalar", mk[pk])
                nk = ("norm", prevside[0], prevside[1], nt)
                if nk in mk:
                    W("scalar", mk[nk])
                sq_last = None
                for c in range(8):
                    sq_last = p.op("scalar", lambda e, c=c, nt=nt, buf=buf: e.activation(
                        q2buf[:, 512 * c:512 * c + 512],
                        buf[:, 1024 * c + 512 * nt:1024 * c + 512 * nt + 512],
                        AF.Square), inc=s_scl)
                W("tensor", sq_last)
                for c in range(8):
                    pe_ss = p.op("tensor", lambda e, c=c: e.matmul(
                        pbank[4][0:16, :], Et_sb[:, 16 * c:16 * c + 16],
                        q2buf[:, 512 * c:512 * c + 512],
                        start=(c == 0), stop=(c == 7)),
                        inc=s_pe if c == 7 else None)
                mk[("ssqpe", side, mt, nt)] = pe_ss
                W("scalar", pe_ss)
                sc_n = p.op("scalar", lambda e, nt=nt: e.activation(
                    normf[:, 512 * nt:512 * nt + 512], pbank[4][0:16, :], AF.Sqrt),
                    inc=s_scl)
                W("vector", sc_n)
                pj = ("projpe", prevside[0], prevside[1])
                if pj in mk:
                    W("vector", mk[pj])
                vneg = p.op("vector", lambda e, nt=nt: e.tensor_scalar_mul(
                    normn[0:16, 512 * nt:512 * nt + 512],
                    normf[:, 512 * nt:512 * nt + 512], -1.0), inc=s_vec)
                mk[("norm", side, mt, nt)] = vneg
            for c in range(8):
                for nt in range(NT2):
                    gi = c * NT2 + nt
                    par = pbank[2 + gi % 2]
                    key = ("fexp", side, mt, gi - 2)
                    if key in mk:
                        W("tensor", mk[key])
                    elif gi < 2:
                        pk = ("fexp", "k" if side == "q" else "q",
                              mt if side == "q" else mt - 1, 8 * NT2 - 2 + gi)
                        if pk in mk:
                            W("tensor", mk[pk])
                    W("tensor", mk[("norm", side, mt, nt)])
                    p.op("tensor", lambda e, par=par, c=c, nt=nt, buf=buf: e.matmul(
                        par[:], rfbd[:],
                        buf[:, 1024 * c + 512 * nt:1024 * c + 512 * nt + 512],
                        start=True, stop=False))
                    pe_pr = p.op("tensor", lambda e, par=par, nt=nt, c=c: e.matmul(
                        par[:], E_sb[:, 128 * c:128 * c + 128],
                        normn[0:16, 512 * nt:512 * nt + 512],
                        start=False, stop=True), inc=s_pe)
                    W("scalar", pe_pr)
                    ex = p.op("scalar", lambda e, par=par, c=c, nt=nt, buf=buf: e.activation(
                        buf[:, 1024 * c + 512 * nt:1024 * c + 512 * nt + 512],
                        par[:], AF.Exp), inc=s_scl)
                    mk[("fexp", side, mt, gi)] = ex
            mk[("projpe", side, mt)] = p.mark(s_pe)
            for c in range(8):
                W("scalar", mk[("fexp", side, mt, c * NT2 + NT2 - 1)])
                p.dma("scalar", hbm[128 * c:128 * c + 128, tb:tb + MTT],
                      buf[:, 1024 * c:1024 * c + MTT], s_st)
            if side == "k":
                mk[("kst", mt)] = p.mark(s_st)
        mk[("kpv_st", mt)] = p.mark(s_st)
        if stage <= 0.9:
            mk[("axfm", mt)] = p.mark(s_pe)
            continue

        W("sync", mk[("kst", mt)])
        for tt in range(MTT // 128):
            t0 = tb + 128 * tt
            ki = tt % 2
            key = ("kvpe", mt, tt - 2)
            if key in mk:
                W("sync", mk[key])
            elif mt > 0 and tt < 2:
                W("sync", mk[("kvpe", mt - 1, MTT // 128 - 2 + tt)])
            p.dma("sync", kvload[:, 2048 * ki:2048 * ki + 1024],
                  kp_h[:, t0:t0 + 128], s_ld, transpose=True)
            l2 = p.dma("sync", kvload[:, 2048 * ki + 1024:2048 * ki + 2048],
                       v_h[:, t0:t0 + 128], s_ld, transpose=True)
            W("tensor", l2)
            first = (mt % 4 == 0) and tt == 0
            if first and mt == 4:
                W("tensor", mk[("kvev", 0)])
            for hh in range(8):
                bank = pbank[5] if hh < 4 else pbank[6]
                p.op("tensor", lambda e, bank=bank, hh=hh, ki=ki, first=first: e.matmul(
                    bank[:, 128 * (hh % 4):128 * (hh % 4) + 128],
                    kvload[:, 2048 * ki + 128 * hh:2048 * ki + 128 * hh + 128],
                    kvload[:, 2048 * ki + 1024 + 128 * hh:2048 * ki + 1024 + 128 * hh + 128],
                    start=(first and hh % 4 == 0), stop=False, skip_group_check=True))
                tok = p.op("tensor", lambda e, hh=hh, ki=ki, first=first: e.matmul(
                    pbank[7][:, hh:hh + 1],
                    kvload[:, 2048 * ki + 128 * hh:2048 * ki + 128 * hh + 128],
                    ones_b[:, 0:1],
                    start=(first and hh == 0), stop=False, skip_group_check=True),
                    inc=s_pe if hh == 7 else None)
            mk[("kvpe", mt, tt)] = tok
        if mt % 4 == 3:
            W("vector", mk[("kvpe", mt, MTT // 128 - 1)])
            for hh in range(8):
                bank = pbank[5] if hh < 4 else pbank[6]
                base = (b * 8 + hh) * 128
                p.op("vector", lambda e, bank=bank, hh=hh, base=base: e.tensor_copy(
                    kvstage[0:64, base:base + 64],
                    bank[0:64, 128 * (hh % 4):128 * (hh % 4) + 64]))
                p.op("vector", lambda e, bank=bank, hh=hh, base=base: e.tensor_copy(
                    kvstage[64:128, base + 64:base + 128],
                    bank[64:128, 128 * (hh % 4) + 64:128 * (hh % 4) + 128]))
                blk = (b * 8 + hh) * 16
                p.op("vector", lambda e, hh=hh, blk=blk: e.tensor_copy(
                    kswide[0:64, blk + 2 * hh:blk + 2 * hh + 1],
                    pbank[7][0:64, hh:hh + 1]))
                p.op("vector", lambda e, hh=hh, blk=blk: e.tensor_copy(
                    kswide[64:128, blk + 2 * hh + 1:blk + 2 * hh + 2],
                    pbank[7][64:128, hh:hh + 1]))
            mk[("kvev", b)] = p.op("vector", lambda e: e.engine_nop(), inc=s_vec)
        mk[("axfm", mt)] = p.mark(s_pe)
    if stage <= 1:
        return _cut(nc, p, [s_ld, s_st, s_gp])

    p.barrier([s_ld, s_st, s_gp, s_pe, s_vec, s_scl])
    W("tensor", mk[("kvev", 1)])
    W("tensor", mk[("yev", NMT - 1, 24 * NT2 - 1)])
    W("tensor", mk[("fexp", "k", NMT - 1, 8 * NT2 - 1)])
    W("sync", mk[("kvpe", NMT - 1, MTT // 128 - 1)])
    W("vector", mk[("kpv_st", NMT - 1)])
    W("scalar", mk[("kpv_st", NMT - 1)])
    sc_sq_last = None
    for nt in range(T // 512):
        b = nt // 8
        t0 = 512 * nt
        qi = nt % 2
        key = ("ntpe", nt - 2)
        if key in mk:
            W("sync", mk[key])
        for c in range(8):
            p.dma("sync", qtile[:, 4096 * qi + 512 * c:4096 * qi + 512 * c + 512],
                  qp_h[128 * c:128 * c + 128, t0:t0 + 512], s_ld)
        t_q = p.mark(s_ld)
        key = ("xrv", nt - 1)
        if key in mk:
            W("sync", mk[key])
        for st_ in range(2):
            p.dma("sync", xres[:, 1024 * st_:1024 * st_ + 1024],
                  x_in[t0 + 128 * st_:t0 + 128 * st_ + 128, :], s_ld)
        t_xr = p.mark(s_ld)

        W("tensor", t_q)
        W("tensor", mk[("kvev", b)])
        key = ("nhv", nt - 1)
        if key in mk:
            W("tensor", mk[key])
        for hh in range(8):
            pe_nh = p.op("tensor", lambda e, hh=hh, b=b, qi=qi: e.matmul(
                pbank[2][0:16, :], kswide[:, (b * 8 + hh) * 16:(b * 8 + hh) * 16 + 16],
                qtile[:, 4096 * qi + 512 * hh:4096 * qi + 512 * hh + 512],
                start=(hh == 0), stop=(hh == 7)),
                inc=s_pe if hh == 7 else None)
        W("vector", pe_nh)
        vv = p.op("vector", lambda e: e.tensor_copy(
            nh_sb[:], pbank[2][0:16, :]), inc=s_vec)
        mk[("nhv", nt)] = vv
        p.op("vector", lambda e: e.tensor_scalar_add(nh_sb[:], nh_sb[:], 1e-6))
        key = ("ntpe", nt - 1)
        if key in mk:
            W("vector", mk[key])
        p.vsync(s_vec)
        p.op("vector", lambda e: e.reciprocal(nh_sb[:], nh_sb[:]))
        p.vsync(s_vec)
        v_s = p.op("vector", lambda e: e.tensor_copy(s_bf[:], nh_sb[:]), inc=s_vec)
        sc_sbc = None
        sc_out = {}
        for c in range(8):
            W("tensor", v_s)
            if sc_sbc is not None:
                W("tensor", sc_sbc)
            pe_sbc = p.op("tensor", lambda e, c=c: e.matmul(
                pbank[3][:], E_sb[:, 128 * c:128 * c + 128], s_bf[:],
                start=True, stop=True), inc=s_pe)
            W("scalar", pe_sbc)
            if c >= 2:
                W("scalar", mk[("vq", nt, c - 2)])
            sc_sbc = p.op("scalar", lambda e, c=c: e.copy(
                sbc_sb[:, 512 * (c % 2):512 * (c % 2) + 512], pbank[3][:]), inc=s_scl)
            W("vector", sc_sbc)
            vq = p.op("vector", lambda e, c=c, qi=qi: e.tensor_mul(
                qtile[:, 4096 * qi + 512 * c:4096 * qi + 512 * c + 512],
                qtile[:, 4096 * qi + 512 * c:4096 * qi + 512 * c + 512],
                sbc_sb[:, 512 * (c % 2):512 * (c % 2) + 512]), inc=s_vec)
            mk[("vq", nt, c)] = vq
            W("tensor", vq)
            if c >= 2:
                W("tensor", sc_out[c - 2])
            par = pbank[c % 2]
            pe_out = p.op("tensor", lambda e, par=par, c=c, b=b, qi=qi: e.matmul(
                par[:], kvstage[:, (b * 8 + c) * 128:(b * 8 + c) * 128 + 128],
                qtile[:, 4096 * qi + 512 * c:4096 * qi + 512 * c + 512],
                start=True, stop=True), inc=s_pe)
            W("scalar", pe_out)
            key = ("ntpe", nt - 2)
            if key in mk:
                W("scalar", mk[key])
            elif nt < 2 and c == 0:
                W("scalar", mk[("kpv_st", NMT - 1)])
            sc_out[c] = p.op("scalar", lambda e, par=par, c=c, qi=qi: e.copy(
                outsc[:, 4096 * qi + 512 * c:4096 * qi + 512 * c + 512], par[:]),
                inc=s_scl)
        out_done = p.mark(s_scl)

        W("tensor", out_done)
        for st_ in range(4):
            for ot in range(2):
                par = pbank[4 + (st_ * 2 + ot) % 4]
                key = ("wev", nt, st_ * 2 + ot - 4)
                if key in mk:
                    W("tensor", mk[key])
                for c in range(8):
                    p.op("tensor", lambda e, par=par, c=c, st_=st_, ot=ot, qi=qi: e.matmul(
                        par[:],
                        outsc[:, 4096 * qi + 512 * c + 128 * st_:
                              4096 * qi + 512 * c + 128 * st_ + 128],
                        w_outT[:, 1024 * c + 512 * ot:1024 * c + 512 * ot + 512],
                        start=(c == 0), stop=False, skip_group_check=True))
                pe_w = p.op("tensor", lambda e, par=par, ot=ot: e.matmul(
                    par[:], ones_b[0:1, 0:128], bout_b[:, 512 * ot:512 * ot + 512],
                    start=False, stop=True, skip_group_check=True), inc=s_pe)
                W("vector", pe_w)
                if st_ < 2:
                    W("vector", t_xr)
                else:
                    W("vector", mk[("xrv2", nt, st_ - 2)])
                if sc_sq_last is not None and ot == 0:
                    W("vector", sc_sq_last)
                vr = p.op("vector", lambda e, par=par, st_=st_, ot=ot: e.tensor_add(
                    y1[:, 512 * ot:512 * ot + 512], par[:],
                    xres[:, 1024 * (st_ % 2) + 512 * ot:1024 * (st_ % 2) + 512 * ot + 512]),
                    inc=s_vec)
                mk[("wev", nt, st_ * 2 + ot)] = vr
            if st_ < 2:
                vr2 = mk[("wev", nt, st_ * 2 + 1)]
                W("sync", vr2)
                ld2 = p.dma("sync", xres[:, 1024 * st_:1024 * st_ + 1024],
                            x_in[t0 + 128 * (st_ + 2):t0 + 128 * (st_ + 2) + 128, :],
                            s_ld)
                mk[("xrv2", nt, st_)] = ld2
            if cy1_d is not None and nt == 0:
                vv1 = p.op("vector", lambda e: e.engine_nop(), inc=s_vec)
                W("scalar", vv1)
                p.dma("scalar", cy1_d[128 * st_:128 * st_ + 128, :], y1[:], s_st)
                p.dma("scalar", cxr_d[128 * st_:128 * st_ + 128, :],
                      xres[:, 1024 * (st_ % 2):1024 * (st_ % 2) + 1024], s_st)
            mi = st_ % 2
            p.op("vector", lambda e: e.reduce_sum(stats[:, 0:1], y1[:], axis=AXX))
            v_m = p.op("vector", lambda e: e.engine_nop(), inc=s_vec)
            W("scalar", v_m)
            sc_sq = p.op("scalar", lambda e: e.activation(
                sqjunk[:], y1[:], AF.Square, accum_out=stats[:, 1:2]), inc=s_scl)
            sc_sq_last = sc_sq
            W("vector", sc_sq)
            p.op("vector", lambda e: e.tensor_scalar(
                stats[:, 2:3], stats[:, 0:1], 1.0 / D, None, ALU.mult))
            p.op("vector", lambda e: e.tensor_scalar(
                stats[:, 3:4], stats[:, 1:2], 1.0 / D, None, ALU.mult))
            p.vsync(s_vec)
            p.op("vector", lambda e: e.tensor_mul(
                stats[:, 4:5], stats[:, 2:3], stats[:, 2:3]))
            p.vsync(s_vec)
            v_var = p.op("vector", lambda e: e.tensor_sub(
                stats[:, 5:6], stats[:, 3:4], stats[:, 4:5]), inc=s_vec)
            W("scalar", v_var)
            sc_std = p.op("scalar", lambda e: e.activation(
                stats[:, 6:7], stats[:, 5:6], AF.Sqrt, bias=eps_sb[:]), inc=s_scl)
            W("vector", sc_std)
            p.op("vector", lambda e: e.reciprocal(stats[:, 7:8], stats[:, 6:7]))
            p.vsync(s_vec)
            if cst_d is not None and nt == 0:
                vvs = p.op("vector", lambda e: e.engine_nop(), inc=s_vec)
                W("scalar", vvs)
                p.dma("scalar", cst_d[128 * st_:128 * st_ + 128, :], stats[:], s_st)
            key = ("xmst", nt, st_ - 2)
            if key in mk:
                W("vector", mk[key])
            v_xm = p.op("vector", lambda e, mi=mi: e.tensor_scalar(
                xmid_sb[:, 1024 * mi:1024 * mi + 1024], y1[:],
                stats[:, 2:3], stats[:, 7:8], ALU.subtract, ALU.mult), inc=s_vec)
            v_xmb = p.op("vector", lambda e, mi=mi: e.tensor_copy(
                xmidb_sb[:, 1024 * mi:1024 * mi + 1024],
                xmid_sb[:, 1024 * mi:1024 * mi + 1024]), inc=s_vec)
            W("scalar", v_xmb)
            p.dma("scalar", xmid32_h[t0 + 128 * st_:t0 + 128 * st_ + 128, :],
                  xmid_sb[:, 1024 * mi:1024 * mi + 1024], s_st)
            st2 = p.dma("scalar", xmidb_h[t0 + 128 * st_:t0 + 128 * st_ + 128, :],
                        xmidb_sb[:, 1024 * mi:1024 * mi + 1024], s_st)
            mk[("xmst", nt, st_)] = st2
        mk[("ntpe", nt)] = p.mark(s_pe)
        mk[("xrv", nt)] = p.mark(s_vec)
    xmid_done = p.mark(s_st)
    pe_c_done = p.mark(s_pe)
    if stage <= 2:
        return _cut(nc, p, [s_ld, s_st, s_gp])

    p.barrier([s_ld, s_st, s_gp, s_pe, s_vec, s_scl])
    W("gpsimd", pe_c_done)
    for ei in range(E):
        for dc in range(8):
            p.dma("gpsimd", W1s[:, (ei * 8 + dc) * 256:(ei * 8 + dc) * 256 + 256],
                  w1_in[ei * 1024 + 128 * dc:ei * 1024 + 128 * dc + 128, :], s_gp)
    for ei in range(E):
        for jc in range(2):
            p.dma("gpsimd", W2s[:, (ei * 2 + jc) * 1024:(ei * 2 + jc) * 1024 + 1024],
                  w2_in[ei * 256 + 128 * jc:ei * 256 + 128 * jc + 128, :], s_gp)
    t_w12 = p.mark(s_gp)

    W("vector", xmid_done)
    W("vector", pe_c_done)
    W("vector", sc_sq_last)

    sft = F32A[:, 2048:2560]
    v_lg_prev = None
    gbuf = [AR_X, AR_C]
    for mt in range(NMT):
        tb = mt * MTT
        W("sync", xmid_done)
        W("sync", pe_c_done)
        if mt > 1:
            W("sync", mk[("dxfm", mt - 2)])
        gb = gbuf[mt % 2]
        for c in range(8):
            p.dma("sync", gb[:, 1024 * c:1024 * (c + 1)],
                  xmidb_h[tb:tb + MTT, 128 * c:128 * c + 128], s_ld, transpose=True)
        t_xfm = p.mark(s_ld)
        if stage <= 2.15:
            mk[("dxfm", mt)] = p.mark(s_pe)
            continue
        W("tensor", t_xfm)
        for nt in range(NT2):
            if stage <= 2.2 and not (mt == 0 and nt == 0):
                continue
            if v_lg_prev is not None:
                W("tensor", v_lg_prev)
            for dc in range(8):
                p.op("tensor", lambda e, dc=dc, nt=nt, gb=gb: e.matmul(
                    pbank[0][0:8, :], wg_b[:, 8 * dc:8 * dc + 8],
                    gb[:, 1024 * dc + 512 * nt:1024 * dc + 512 * nt + 512],
                    start=(dc == 0), stop=False, skip_group_check=True))
            pe_g = p.op("tensor", lambda e: e.matmul(
                pbank[0][0:8, :], wgb_b[:], ones_b[0:1, :],
                start=False, stop=True, skip_group_check=True), inc=s_pe)
            W("vector", pe_g)
            v_lg = p.op("vector", lambda e: e.tensor_copy(lg_sb[:], pbank[0][0:8, :]),
                        inc=s_vec)
            v_lg_prev = v_lg
            grp = mt * NT2 + nt
            gb2 = pbank[1 + grp % 2]
            W("tensor", v_lg)
            key = ("gtr", grp - 2)
            if key in mk:
                W("tensor", mk[key])
            for st_ in range(4):
                pe_tr = p.op("tensor", lambda e, gb2=gb2, st_=st_: e.transpose(
                    gb2[:, 8 * st_:8 * st_ + 8], lg_sb[:, 128 * st_:128 * st_ + 128],
                    ident_f[0:8, 0:8]), inc=s_pe if st_ == 3 else None)
            W("scalar", pe_tr)
            key = ("gmsk", grp - 1)
            if key in mk:
                W("scalar", mk[key])
            sc_e = p.op("scalar", lambda e, gb2=gb2: e.activation(
                sft[:, 0:32], gb2[:, 0:32], AF.Exp), inc=s_scl)
            mk[("gtr", grp)] = sc_e
            W("vector", sc_e)
            for st_ in range(4):
                p.op("vector", lambda e, st_=st_: e.reduce_sum(
                    sft[:, 32 + st_:33 + st_], sft[:, 8 * st_:8 * st_ + 8], axis=AXX))
            for st_ in range(4):
                p.op("vector", lambda e, st_=st_: e.reciprocal(
                    sft[:, 36 + st_:37 + st_], sft[:, 32 + st_:33 + st_]))
            for st_ in range(4):
                p.op("vector", lambda e, st_=st_: e.tensor_scalar_mul(
                    sft[:, 40 + 8 * st_:48 + 8 * st_], sft[:, 8 * st_:8 * st_ + 8],
                    sft[:, 36 + st_:37 + st_]))
            for st_ in range(4):
                p.op("vector", lambda e, st_=st_: e.max(
                    out=sft[:, 72 + 8 * st_:80 + 8 * st_],
                    in_=sft[:, 40 + 8 * st_:48 + 8 * st_]))
            for st_ in range(4):
                p.op("vector", lambda e, st_=st_: e.tensor_scalar(
                    sft[:, 104 + 8 * st_:112 + 8 * st_],
                    sft[:, 40 + 8 * st_:48 + 8 * st_],
                    sft[:, 73 + 8 * st_:74 + 8 * st_], None, ALU.is_ge))
            v_msk = None
            for st_ in range(4):
                g = (tb + 512 * nt) // 128 + st_
                v_msk = p.op("vector", lambda e, g=g, st_=st_: e.tensor_mul(
                    masked_all[:, 8 * g:8 * g + 8],
                    sft[:, 40 + 8 * st_:48 + 8 * st_],
                    sft[:, 104 + 8 * st_:112 + 8 * st_]),
                    inc=s_vec if st_ == 3 else None)
            mk[("gmsk", grp)] = v_msk
        mk[("dxfm", mt)] = p.mark(s_pe)

    if stage <= 2.3:
        return _cut(nc, p, [s_ld, s_st, s_gp])
    for i in range(32):
        p.op("vector", lambda e, i=i: e.tensor_add(
            part_sb[:, 8 * i:8 * i + 8], masked_all[:, 8 * i:8 * i + 8],
            masked_all[:, 8 * (i + 32):8 * (i + 32) + 8]))
    v_part = p.op("vector", lambda e: e.engine_nop(), inc=s_vec)
    W("scalar", v_part)
    st_cc = p.dma("scalar", cc_in[:], part_sb[:], s_st)
    W("gpsimd", st_cc)
    cc_done = p.op("gpsimd", lambda e: e.collective_compute(
        "AllReduce", ALU.add, replica_groups=[list(range(8))],
        ins=[cc_in[:]], outs=[cc_out[:]]), inc=s_cc)
    W("gpsimd", cc_done)
    t_den = p.dma("gpsimd", den_sb[:], cc_out[:], s_gp)
    W("vector", t_den)
    p.op("vector", lambda e: e.tensor_scalar_add(den_sb[:], den_sb[:], 1e-6))
    p.vsync(s_vec)
    p.op("vector", lambda e: e.reciprocal(den_sb[:], den_sb[:]))
    p.vsync(s_vec)
    v_den = p.op("vector", lambda e: e.tensor_scalar_mul(den_sb[:], den_sb[:], 16.0),
                 inc=s_vec)
    if stage <= 2.6:
        return _cut(nc, p, [s_ld, s_st, s_gp, s_cc])

    exfm_t = {}

    def _eload(mt):
        tb = mt * MTT
        if mt > 0:
            W("sync", mk[("e1pe", mt - 1)])
        else:
            W("sync", mk[("dxfm", NMT - 1)])
        for c in range(8):
            p.dma("sync", AR_X[:, 1024 * c:1024 * (c + 1)],
                  xmidb_h[tb:tb + MTT, 128 * c:128 * c + 128], s_ld, transpose=True)
        exfm_t[mt] = p.mark(s_ld)

    def _e1(mt):
        W("tensor", exfm_t[mt])
        if mt == 0:
            W("tensor", t_w12)
            W("tensor", v_lg_prev)
            W("tensor", mk[("gtr", 14)])
            W("tensor", mk[("gtr", 15)])
        for ei in range(E):
            for half in range(2):
                g2 = ei * 2 + half
                bankset = (0, 1) if g2 % 2 == 0 else (2, 3)
                for nt in range(NT2):
                    key = ("hevg", mt, (g2 - 2) * NT2 + nt)
                    if key in mk:
                        W("tensor", mk[key])
                    elif mt > 0 and g2 == 0:
                        W("tensor", mk[("hevg", mt - 1, 14 * NT2 + nt)])
                    elif mt > 0 and g2 == 1:
                        W("tensor", mk[("sbev", mt - 1, 6 + nt)])
                for dc in range(8):
                    for nt in range(NT2):
                        pe_h = p.op("tensor", lambda e, bankset=bankset, nt=nt, dc=dc, ei=ei, half=half: e.matmul(
                            pbank[bankset[nt]][:],
                            W1s[:, (ei * 8 + dc) * 256 + 128 * half:
                                (ei * 8 + dc) * 256 + 128 * half + 128],
                            AR_X[:, 1024 * dc + 512 * nt:1024 * dc + 512 * nt + 512],
                            start=(dc == 0), stop=(dc == 7)),
                            inc=s_pe if (dc == 7 and nt == NT2 - 1) else None)
                W("scalar", pe_h)
                if mt > 0 and g2 == 0:
                    W("scalar", mk[("e2pe", mt - 1)])
                for nt in range(NT2):
                    col = g2 * 1024 + 512 * nt
                    sc_g = p.op("scalar", lambda e, bankset=bankset, nt=nt, col=col, ei=ei, half=half: e.activation(
                        hgm[:, col:col + 512], pbank[bankset[nt]][:], AF.Gelu,
                        bias=b1s[:, 8 * half + ei:8 * half + ei + 1]), inc=s_scl)
                    mk[("hevg", mt, g2 * NT2 + nt)] = sc_g
        mk[("e1pe", mt)] = p.mark(s_pe)

    _eload(0)
    _e1(0)

    W("tensor", mk[("gtr", 15)])
    W("tensor", mk[("gtr", 14)])
    rta = F32A[:, 2560:3072]
    v_rtall = None
    for i in range(32):
        p.op("vector", lambda e, i=i: e.tensor_mul(
            rta[:, 16 * i:16 * i + 8], masked_all[:, 8 * i:8 * i + 8],
            den_sb[:, 8 * i:8 * i + 8]))
        v_rtall = p.op("vector", lambda e, i=i: e.tensor_mul(
            rta[:, 16 * i + 8:16 * i + 16],
            masked_all[:, 8 * (i + 32):8 * (i + 32) + 8],
            den_sb[:, 8 * i:8 * i + 8]), inc=s_vec if i == 31 else None)
    W("tensor", v_rtall)
    v_pm_b = None
    for half in range(2):
        if half == 1:
            W("tensor", v_pm_b)
        for j in range(16):
            i = 16 * half + j
            par = pbank[4 + j // 4]
            col = 128 * (j % 4)
            pe_rt = p.op("tensor", lambda e, par=par, col=col, i=i: e.transpose(
                par[0:16, col:col + 128], rta[:, 16 * i:16 * i + 16], ident_f[:]),
                inc=s_pe if j == 15 else None)
        W("vector", pe_rt)
        for j in range(16):
            i = 16 * half + j
            par = pbank[4 + j // 4]
            col = 128 * (j % 4)
            v_pm_b = p.op("vector", lambda e, par=par, col=col, i=i: e.tensor_copy(
                route_pm[:, 128 * i:128 * i + 128], par[0:16, col:col + 128]),
                inc=s_vec if j == 15 else None)
    if stage <= 2.8:
        return _cut(nc, p, [s_ld, s_st, s_gp, s_cc])
    LO, HI, TH, CN, CD, NE, CH = range(7)
    p.op("vector", lambda e: e.memset(bis[:, LO:LO + 1], -1.0))
    p.op("vector", lambda e: e.memset(bis[:, HI:HI + 1], 16.5))
    for it in range(BIS_ITERS):
        p.vsync(s_vec)
        p.op("vector", lambda e: e.tensor_add(
            bis[:, TH:TH + 1], bis[:, LO:LO + 1], bis[:, HI:HI + 1]))
        p.vsync(s_vec)
        p.op("vector", lambda e: e.tensor_scalar_mul(
            bis[:, TH:TH + 1], bis[:, TH:TH + 1], 0.5))
        p.vsync(s_vec)
        p.op("vector", lambda e: e.tensor_scalar(
            junk_bf[:], route_pm[:], bis[:, TH:TH + 1], None,
            ALU.is_gt, ALU.add, accum_out=bis[:, CN:CN + 1]))
        p.vsync(s_vec)
        p.op("vector", lambda e: e.tensor_scalar(
            bis[:, CD:CD + 1], bis[:, CN:CN + 1], float(KCAP), None, ALU.is_ge))
        p.vsync(s_vec)
        p.op("vector", lambda e: e.tensor_scalar(
            bis[:, 7:8], bis[:, CD:CD + 1], -1.0, 1.0, ALU.mult, ALU.add))
        p.op("vector", lambda e: e.tensor_mul(
            bis[:, 8:9], bis[:, CD:CD + 1], bis[:, TH:TH + 1]))
        p.vsync(s_vec)
        p.op("vector", lambda e: e.tensor_mul(
            bis[:, 9:10], bis[:, 7:8], bis[:, LO:LO + 1]))
        p.op("vector", lambda e: e.tensor_mul(
            bis[:, 10:11], bis[:, CD:CD + 1], bis[:, HI:HI + 1]))
        p.op("vector", lambda e: e.tensor_mul(
            bis[:, 11:12], bis[:, 7:8], bis[:, TH:TH + 1]))
        p.vsync(s_vec)
        p.op("vector", lambda e: e.tensor_add(
            bis[:, LO:LO + 1], bis[:, 8:9], bis[:, 9:10]))
        p.op("vector", lambda e: e.tensor_add(
            bis[:, HI:HI + 1], bis[:, 10:11], bis[:, 11:12]))
    p.vsync(s_vec)
    p.op("vector", lambda e: e.tensor_scalar(
        selhi[:], route_pm[:], bis[:, HI:HI + 1], None,
        ALU.is_gt, ALU.add, accum_out=bis[:, CH:CH + 1]))
    p.vsync(s_vec)
    p.op("vector", lambda e: e.tensor_scalar(
        bis[:, NE:NE + 1], bis[:, CH:CH + 1], -1.0, float(KCAP), ALU.mult, ALU.add))
    p.vsync(s_vec)
    p.op("vector", lambda e: e.tensor_scalar(
        cand_a[:], route_pm[:], bis[:, LO:LO + 1], None, ALU.is_gt))
    p.op("vector", lambda e: e.tensor_scalar(
        cand_b[:], route_pm[:], bis[:, HI:HI + 1], None, ALU.is_le))
    p.op("vector", lambda e: e.tensor_mul(cand_a[:], cand_a[:], cand_b[:]))
    p.op("vector", lambda e: e.tensor_tensor_scan(
        tier[:], cand_a[:], cand_a[:], 0.0, ALU.add, ALU.max))
    p.op("vector", lambda e: e.tensor_scalar(
        cand_b[:], tier[:], bis[:, NE:NE + 1], None, ALU.is_le))
    p.op("vector", lambda e: e.tensor_mul(cand_a[:], cand_a[:], cand_b[:]))
    v_sel = p.op("vector", lambda e: e.tensor_add(selpm_t[:], selhi[:], cand_a[:]),
                 inc=s_vec)
    W("gpsimd", v_sel)
    t_s0 = p.dma("gpsimd", sel_b0[:], selpm_t[0:8, :], s_gp)
    t_s1 = p.dma("gpsimd", sel_b1[:], selpm_t[8:16, :], s_gp)
    if route_d is not None:
        W("scalar", v_sel)
        p.dma("scalar", route_d[:], route_pm[:], s_st)
    if sel_d is not None:
        W("scalar", v_sel)
        p.dma("scalar", sel_d[0:8, :], selpm_t[0:8, :], s_st)
        p.dma("scalar", sel_d[8:16, :], selpm_t[8:16, :], s_st)

    if stage <= 3:
        return _cut(nc, p, [s_ld, s_st, s_gp, s_cc])

    out_final = None
    for mt in range(NMT):
        b = mt // 4
        tb = mt * MTT
        nb = tb - b * N
        selb = sel_b0 if b == 0 else sel_b1
        if mt > 0:
            _e1(mt)
        if mt + 1 < NMT:
            _eload(mt + 1)

        for ei in range(E):
            par = pbank[2 + ei % 2]
            key = ("sbev", mt, ei - 2)
            if key in mk:
                W("tensor", mk[key])
            if ei < 2:
                W("tensor", mk[("hevg", mt, 15 * NT2 + ei)])
            if mt > 0 and ei < 2:
                W("tensor", mk[("eev", mt - 1, MTT // 128 - 1)])
            if mt == 0 and ei == 0:
                W("tensor", t_s0)
                W("tensor", t_s1)
            pe_sb = p.op("tensor", lambda e, par=par, ei=ei, nb=nb, selb=selb: e.matmul(
                par[:], rowsel[:, 128 * ei:128 * ei + 128],
                selb[:, nb:nb + 512], start=True, stop=True), inc=s_pe)
            par2 = pbank[4 + ei % 2]
            key = ("sbev2", mt, ei - 2)
            if key in mk:
                W("tensor", mk[key])
            pe_sb2 = p.op("tensor", lambda e, par2=par2, ei=ei, nb=nb, selb=selb: e.matmul(
                par2[:], rowsel[:, 128 * ei:128 * ei + 128],
                selb[:, nb + 512:nb + 1024], start=True, stop=True), inc=s_pe)
            W("scalar", pe_sb)
            sc_a = p.op("scalar", lambda e, par=par, ei=ei: e.copy(
                selbc[:, 1024 * ei:1024 * ei + 512], par[:]), inc=s_scl)
            mk[("sbev", mt, ei)] = sc_a
            W("scalar", pe_sb2)
            sc_b = p.op("scalar", lambda e, par2=par2, ei=ei: e.copy(
                selbc[:, 1024 * ei + 512:1024 * ei + 1024], par2[:]), inc=s_scl)
            mk[("sbev2", mt, ei)] = sc_b
        selbc_done = p.mark(s_scl)

        for g2 in range(16):
            ei = g2 // 2
            for nt in range(NT2):
                col = g2 * 1024 + 512 * nt
                W("vector", mk[("hevg", mt, g2 * NT2 + nt)])
                W("vector", mk[("sbev2" if nt else "sbev", mt, ei)])
                vm = p.op("vector", lambda e, col=col, ei=ei, nt=nt: e.tensor_mul(
                    hgm[:, col:col + 512], hgm[:, col:col + 512],
                    selbc[:, 1024 * ei + 512 * nt:1024 * ei + 512 * nt + 512]),
                    inc=s_vec)
                mk[("hev", mt, g2 * NT2 + nt)] = vm
        h_done = p.mark(s_vec)
        if hg_d is not None and sbc_d is not None and mt == 0:
            W("scalar", h_done)
            p.dma("scalar", hg_d[:], hgm[:], s_st)
            p.dma("scalar", sbc_d[:], selbc[:], s_st)

        W("tensor", h_done)
        for st_ in range(MTT // 128):
            t0 = tb + 128 * st_
            n0 = nb + 128 * st_
            pidx = st_ % 2
            banks = (4, 5) if pidx == 0 else (6, 7)
            key = ("eev", mt, st_ - 2)
            if key in mk:
                W("tensor", mk[key])
            for ot in range(2):
                for kk in range(16):
                    ei, half = kk // 2, kk % 2
                    p.op("tensor", lambda e, banks=banks, ot=ot, ei=ei, half=half, st_=st_, kk=kk: e.matmul(
                        pbank[banks[ot]][:],
                        hgm[:, (ei * 2 + half) * 1024 + 128 * st_:
                            (ei * 2 + half) * 1024 + 128 * st_ + 128],
                        W2s[:, (ei * 2 + half) * 1024 + 512 * ot:
                            (ei * 2 + half) * 1024 + 512 * ot + 512],
                        start=(kk == 0), stop=False, skip_group_check=True))
                pe_e2 = p.op("tensor", lambda e, banks=banks, ot=ot, selb=selb, n0=n0: e.matmul(
                    pbank[banks[ot]][:], selb[:, n0:n0 + 128],
                    b2s[:, 512 * ot:512 * ot + 512],
                    start=False, stop=True, skip_group_check=True),
                    inc=s_pe if ot == 1 else None)
            if ps_d is not None and mt == 0:
                W("scalar", pe_e2)
                for ot in range(2):
                    ps_ev = p.op("scalar", lambda e, banks=banks, ot=ot: e.copy(
                        sqjunk[:, 512 * ot:512 * ot + 512] if False else
                        F32A[:, 3072 + 512 * ot:3072 + 512 * ot + 512],
                        pbank[banks[ot]][:]), inc=s_scl)
                W("scalar", ps_ev)
                p.dma("scalar", ps_d[128 * st_:128 * st_ + 128, :],
                      F32A[:, 3072:4096], s_st)
            key = ("hpv", mt, st_ - 2)
            if key in mk:
                W("scalar", mk[key])
            elif mt > 0 and st_ < 2:
                W("scalar", mk[("hpv", mt - 1, MTT // 128 - 2 + st_)])
            if mt == 0 and st_ < 2:
                W("scalar", v_sel)
            t_hp = p.dma("scalar", hpad[:, 1024 * pidx:1024 * pidx + 1024],
                         xmid32_h[t0:t0 + 128, :], s_st)
            W("vector", pe_e2)
            W("vector", t_hp)
            if sc_sq_last is not None:
                W("vector", sc_sq_last)
            for ot in range(2):
                p.op("vector", lambda e, banks=banks, ot=ot, pidx=pidx: e.tensor_add(
                    y1[:, 512 * ot:512 * ot + 512], pbank[banks[ot]][:],
                    hpad[:, 1024 * pidx + 512 * ot:1024 * pidx + 512 * ot + 512]))
            v_y2 = p.op("vector", lambda e: e.engine_nop(), inc=s_vec)
            if y1_d is not None and mt == 0:
                W("scalar", v_y2)
                p.dma("scalar", y1_d[128 * st_:128 * st_ + 128, :], y1[:], s_st)
            mk[("hpv", mt, st_)] = v_y2
            mk[("eev", mt, st_)] = v_y2
            p.op("vector", lambda e: e.reduce_sum(stats[:, 0:1], y1[:], axis=AXX))
            v_m = p.op("vector", lambda e: e.engine_nop(), inc=s_vec)
            W("scalar", v_m)
            if mt == 0 and st_ == 0:
                W("scalar", v_sel)
            sc_sq = p.op("scalar", lambda e: e.activation(
                sqjunk[:], y1[:], AF.Square, accum_out=stats[:, 1:2]), inc=s_scl)
            sc_sq_last = sc_sq
            W("vector", sc_sq)
            p.op("vector", lambda e: e.tensor_scalar(
                stats[:, 2:3], stats[:, 0:1], 1.0 / D, None, ALU.mult))
            p.op("vector", lambda e: e.tensor_scalar(
                stats[:, 3:4], stats[:, 1:2], 1.0 / D, None, ALU.mult))
            p.vsync(s_vec)
            p.op("vector", lambda e: e.tensor_mul(
                stats[:, 4:5], stats[:, 2:3], stats[:, 2:3]))
            p.vsync(s_vec)
            v_var = p.op("vector", lambda e: e.tensor_sub(
                stats[:, 5:6], stats[:, 3:4], stats[:, 4:5]), inc=s_vec)
            W("scalar", v_var)
            sc_std = p.op("scalar", lambda e: e.activation(
                stats[:, 6:7], stats[:, 5:6], AF.Sqrt, bias=eps_sb[:]), inc=s_scl)
            W("vector", sc_std)
            p.op("vector", lambda e: e.reciprocal(stats[:, 7:8], stats[:, 6:7]))
            p.vsync(s_vec)
            key = ("yst", mt, st_ - 2)
            if key in mk:
                W("vector", mk[key])
            if mt == 0 and st_ < 2:
                W("vector", v_sel)
            v_out = p.op("vector", lambda e, pidx=pidx: e.tensor_scalar(
                yout_sb[:, 1024 * pidx:1024 * pidx + 1024], y1[:],
                stats[:, 2:3], stats[:, 7:8], ALU.subtract, ALU.mult), inc=s_vec)
            W("sync", v_out)
            out_final = p.dma("sync", y_out[t0:t0 + 128, :],
                              yout_sb[:, 1024 * pidx:1024 * pidx + 1024], s_out)
            mk[("yst", mt, st_)] = out_final
        mk[("exfm", mt)] = p.mark(s_pe)
        mk[("e2pe", mt)] = p.mark(s_pe)

    W("sync", out_final)
    W("gpsimd", out_final)

    return _emit(nc, p)


def _emit(nc, p):
    with nc.Block() as block:
        @block.sync
        def _(e):
            for fn in p.ops["sync"]:
                fn(e)

        @block.scalar
        def _(e):
            for fn in p.ops["scalar"]:
                fn(e)

        @block.vector
        def _(e):
            for fn in p.ops["vector"]:
                fn(e)

        @block.tensor
        def _(e):
            for fn in p.ops["tensor"]:
                fn(e)

        @block.gpsimd
        def _(e):
            for fn in p.ops["gpsimd"]:
                fn(e)

    return nc


def _cut(nc, p, sems):
    for s in sems:
        p.wait("sync", p.mark(s))
    return _emit(nc, p)


def _host_inputs(inputs, core):
    x = np.ascontiguousarray(
        np.asarray(inputs["L0_attn_x"])[2 * core:2 * core + 2].reshape(T, D))
    emat = np.zeros((16, D), dtype=np.float32)
    for h in range(16):
        emat[h, 64 * h:64 * h + 64] = 1.0
    rowsel = np.zeros((E, E * 128), dtype=np.float32)
    for e in range(E):
        rowsel[e, 128 * e:128 * e + 128] = 1.0
    m = {
        "x": x,
        "w_qkv": inputs["L0_attn_w_qkv"],
        "w_out": inputs["L0_attn_w_out"],
        "b_out": np.asarray(inputs["L0_attn_b_out"]).reshape(1, D),
        "rf": inputs["L0_attn_random_features"],
        "w_gate": inputs["L0_ffn_w_gate_w"],
        "w_gate_b": np.asarray(inputs["L0_ffn_w_gate_b"]).reshape(1, E),
        "W1": np.asarray(inputs["L0_ffn_W1"]).reshape(E * D, SH),
        "b1": inputs["L0_ffn_b1"],
        "W2": np.asarray(inputs["L0_ffn_W2"]).reshape(E * SH, D),
        "b2": inputs["L0_ffn_b2"],
        "ident128": np.eye(128, dtype=np.float32),
        "onesM": np.ones((128, 512), dtype=np.float32),
        "Emat": emat,
        "Etmat": np.ascontiguousarray(emat.T),
        "Rowsel": rowsel,
    }
    return {k: np.ascontiguousarray(np.asarray(v, dtype=np.float32)) for k, v in m.items()}


def kernel(**inputs):
    nc = build_nc()
    in_maps = [_host_inputs(inputs, c) for c in range(8)]
    res = run_bass_kernel_spmd(nc, in_maps, core_ids=list(range(8)))
    out = np.concatenate([r["y"].reshape(B2, N, D) for r in res.results], axis=0)
    return np.ascontiguousarray(out.astype(np.float32))


# version 17 (final)
# speedup vs baseline: 1.0133x; 1.0133x over previous
import numpy as np
import concourse.bass as bass
import concourse.mybir as mybir
from concourse.bass_utils import run_bass_kernel_spmd

F32 = mybir.dt.float32
BF16 = mybir.dt.bfloat16
AF = mybir.ActivationFunctionType
ALU = mybir.AluOpType
AXX = mybir.AxisListType.X

B2 = 2
N = 4096
T = B2 * N
D = 1024
E = 8
SH = 256
KCAP = 2048
MTT = 1024
NMT = T // MTT
NT2 = MTT // 512
BIS_ITERS = 26

ENGS = ("sync", "scalar", "vector", "tensor", "gpsimd")


class Prog:
    def __init__(self, nc):
        self.nc = nc
        self.ops = {k: [] for k in ENGS}
        self.cnt = {}
        self.watermark = {k: {} for k in ENGS}

    def sem(self, name):
        h = self.nc.alloc_semaphore(name)
        self.cnt[h] = 0
        return h

    def dma(self, q, out, in_, sem, **kw):
        self.cnt[sem] += 16
        tgt = self.cnt[sem]

        def fn(e, out=out, in_=in_, kw=kw, sem=sem):
            e.dma_start(out=out, in_=in_, **kw).then_inc(sem, 16)

        self.ops[q].append(fn)
        return (sem, tgt)

    def op(self, eng, build, inc=None):
        if inc is not None:
            self.cnt[inc] += 1
            tgt = self.cnt[inc]

            def fn(e, build=build, inc=inc):
                build(e).then_inc(inc, 1)

            self.ops[eng].append(fn)
            return (inc, tgt)
        self.ops[eng].append(lambda e, build=build: build(e))
        return None

    def wait(self, eng, tok):
        if tok is None:
            return
        sem, tgt = tok
        if tgt <= self.watermark[eng].get(sem, 0):
            return
        self.watermark[eng][sem] = tgt
        self.ops[eng].append(lambda e, sem=sem, tgt=tgt: e.wait_ge(sem, tgt))

    def mark(self, sem):
        return (sem, self.cnt[sem])

    def vsync(self, s_vec):
        tok = self.op("vector", lambda e: e.engine_nop(), inc=s_vec)
        self.wait("vector", tok)

    def barrier(self, sems):
        toks = [self.mark(s) for s in sems]
        for eng in ENGS:
            for t in toks:
                self.wait(eng, t)


def build_nc(dbg=(), stage=5):
    nc = bass.Bass("TRN2", target_bir_lowering=False, detect_race_conditions=False)
    p = Prog(nc)
    W = p.wait
    M = p.mark

    dt_ = nc.dram_tensor
    x_in = dt_("x", [T, D], F32, kind="ExternalInput").ap()
    wqkv_in = dt_("w_qkv", [3 * D, D], F32, kind="ExternalInput").ap()
    wout_in = dt_("w_out", [D, D], F32, kind="ExternalInput").ap()
    bout_in = dt_("b_out", [1, D], F32, kind="ExternalInput").ap()
    rf_in = dt_("rf", [64, 64], F32, kind="ExternalInput").ap()
    wg_in = dt_("w_gate", [D, E], F32, kind="ExternalInput").ap()
    wgb_in = dt_("w_gate_b", [1, E], F32, kind="ExternalInput").ap()
    w1_in = dt_("W1", [E * D, SH], F32, kind="ExternalInput").ap()
    b1_in = dt_("b1", [E, SH], F32, kind="ExternalInput").ap()
    w2_in = dt_("W2", [E * SH, D], F32, kind="ExternalInput").ap()
    b2_in = dt_("b2", [E, D], F32, kind="ExternalInput").ap()
    ident_in = dt_("ident128", [128, 128], F32, kind="ExternalInput").ap()
    ones_in = dt_("onesM", [128, 512], F32, kind="ExternalInput").ap()
    emat_in = dt_("Emat", [16, D], F32, kind="ExternalInput").ap()
    etmat_in = dt_("Etmat", [D, 16], F32, kind="ExternalInput").ap()
    rowsel_in = dt_("Rowsel", [E, E * 128], F32, kind="ExternalInput").ap()

    y_out = dt_("y", [T, D], F32, kind="ExternalOutput").ap()

    def scratch(name, shape, dtype):
        kind = "ExternalOutput" if name in dbg else "Internal"
        return dt_(name, shape, dtype, kind=kind).ap()

    xb_h = scratch("xb_h", [T, D], BF16)
    wqkv_bh = scratch("wqkv_bh", [3 * D, D], BF16)
    wout_bh = scratch("wout_bh", [D, D], BF16)
    qp_h = scratch("qp_h", [D, T], BF16)
    kp_h = scratch("kp_h", [D, T], BF16)
    v_h = scratch("v_h", [D, T], BF16)
    xmid32_h = scratch("xmid32_h", [T, D], F32)
    xmidb_h = scratch("xmidb_h", [T, D], BF16)
    cc_in = dt_("cc_in", [128, 256], F32, kind="Internal").ap()
    cc_out = dt_("cc_out", [128, 256], F32, kind="Internal", addr_space="Shared").ap()
    route_d = scratch("route_dbg", [16, N], F32) if "route_dbg" in dbg else None
    sel_d = scratch("sel_dbg", [16, N], BF16) if "sel_dbg" in dbg else None
    hg_d = scratch("hg_dbg", [128, 16384], BF16) if "hg_dbg" in dbg else None
    sbc_d = scratch("sbc_dbg", [128, 8192], BF16) if "sbc_dbg" in dbg else None
    y1_d = scratch("y1_dbg", [MTT, D], F32) if "y1_dbg" in dbg else None
    ps_d = scratch("ps_dbg", [MTT, D], F32) if "ps_dbg" in dbg else None
    cy1_d = scratch("cy1_dbg", [512, D], F32) if "cy1_dbg" in dbg else None
    cst_d = scratch("cst_dbg", [512, 16], F32) if "cst_dbg" in dbg else None
    cxr_d = scratch("cxr_dbg", [512, D], F32) if "cxr_dbg" in dbg else None

    sb = nc.alloc_sbuf_tensor
    AR_W = sb("AR_W", [128, 32768], BF16)
    wqkvT = AR_W[:, 0:24576]
    w_outT = AR_W[:, 24576:32768]
    W1s = AR_W[:, 0:16384]
    W2s = AR_W[:, 16384:32768]
    AR_ACT = sb("AR_ACT", [128, 16384], BF16)
    q_bf = AR_ACT[:, 0:8192]
    k_bf = AR_ACT[:, 8192:16384]
    hgm = AR_ACT
    AR_X = sb("AR_X", [128, 8192], BF16)
    AR_C = sb("AR_C", [128, 16384], BF16)
    qtile = AR_C[:, 0:8192]
    outsc = AR_C[:, 8192:16384]
    kvload = AR_C[:, 0:4096]
    q2buf = AR_C[:, 4096:8192]
    vstage = AR_C[:, 8192:9216]
    junk_bf = AR_C[0:16, 0:4096]
    cand_a = AR_C[0:16, 4096:8192]
    cand_b = AR_C[0:16, 8192:12288]
    selhi = AR_C[0:16, 12288:16384]
    selpm_t = AR_C[0:16, 0:4096]
    selbc = AR_C[:, 0:8192]
    F32A = sb("F32A", [128, 6144], F32)
    normf = F32A[0:16, 0:1024]
    b1stage = F32A[0:8, 2048:2304]
    xres = F32A[:, 0:2048]
    y1 = F32A[:, 2048:3072]
    sqjunk = F32A[:, 3072:4096]
    xmid_sb = F32A[:, 4096:6144]
    route_pm = F32A[0:16, 0:4096]
    tier = F32A[0:16, 0:4096]
    den_sb = F32A[:, 4096:4352]
    part_sb = F32A[:, 4352:4608]
    route_tok = F32A[:, 4608:4640]
    lg_sb = F32A[0:8, 5120:5632]
    masked_all = F32A[:, 5632:6144]
    hpad = F32A[:, 0:2048]
    yout_sb = F32A[:, 4096:6144]

    xmidb_sb = AR_ACT[:, 0:2048]
    sbc_sb = AR_ACT[:, 2048:3072]
    s_bf = AR_ACT[0:16, 3072:3584]
    identb = AR_ACT[:, 3584:3712]
    sel_b0 = AR_C[0:8, 8192:12288]
    sel_b1 = AR_C[0:8, 12288:16384]

    ident_f = sb("ident_f", [128, 128], F32)
    ones_b = sb("ones_b", [128, 512], BF16)
    rowsel = sb("rowsel", [8, 1024], BF16)
    rfbd = sb("rfbd", [128, 128], BF16)
    E_sb = sb("E_sb", [16, 1024], BF16)
    Et_sb = sb("Et_sb", [128, 128], BF16)
    wg_b = sb("wg_b", [128, 64], BF16)
    wgb_b = sb("wgb_b", [1, 8], BF16)
    bout_b = sb("bout_b", [1, 1024], BF16)
    b1s = sb("b1s", [128, 16], F32)
    b2s = sb("b2s", [8, 1024], BF16)
    normn = sb("normn", [16, 1024], BF16)
    kvstage = sb("kvstage", [128, 2048], BF16)
    kswide = sb("kswide", [128, 256], BF16)
    nh_sb = sb("nh_sb", [16, 512], F32)
    stats = sb("stats", [128, 16], F32)
    exp_sb = sb("exp_sb", [128, 16], F32)
    gss = sb("gss", [128, 48], F32)
    bis = sb("bis", [16, 12], F32)
    eps_sb = sb("eps_sb", [128, 1], F32)

    pbank = [nc.alloc_psum_tensor(f"pb{i}", [128, 512], F32) for i in range(8)]

    s_ld = p.sem("s_ld")
    s_st = p.sem("s_st")
    s_gp = p.sem("s_gp")
    s_pe = p.sem("s_pe")
    s_vec = p.sem("s_vec")
    s_scl = p.sem("s_scl")
    s_cc = p.sem("s_cc")
    s_out = p.sem("s_out")

    p.dma("gpsimd", ones_b[:], ones_in[:], s_gp)
    p.dma("gpsimd", E_sb[:], emat_in[:], s_gp)
    p.dma("gpsimd", rowsel[:], rowsel_in[:], s_gp)
    for c in range(8):
        p.dma("gpsimd", Et_sb[:, 16 * c:16 * c + 16],
              etmat_in[128 * c:128 * c + 128, :], s_gp)
    for dc in range(8):
        p.dma("gpsimd", wg_b[:, 8 * dc:8 * dc + 8],
              wg_in[128 * dc:128 * dc + 128, :], s_gp)
    p.dma("gpsimd", wgb_b[:], wgb_in[:], s_gp)
    p.dma("gpsimd", bout_b[:], bout_in[:], s_gp)
    p.dma("gpsimd", b2s[:], b2_in[:], s_gp)
    t_consts = p.mark(s_gp)
    t_ident = p.dma("sync", ident_f[:], ident_in[:], s_ld)
    t_b1 = p.dma("sync", b1stage[:], b1_in[:], s_ld)
    rf_f = F32A[0:64, 1024:1088]
    rf_t = p.dma("sync", rf_f, rf_in[:], s_ld)
    t_wload = p.mark(s_ld)

    t_xb = [None]
    for mt in range(1, NMT):
        r0 = mt * MTT
        t_xb.append(p.dma("gpsimd", xb_h[r0:r0 + MTT, :], x_in[r0:r0 + MTT, :], s_gp))

    W("vector", t_ident)
    p.op("vector", lambda e: e.tensor_copy(identb[:], ident_f[:]))
    p.op("vector", lambda e: e.memset(rfbd[:], 0.0))
    p.op("vector", lambda e: e.memset(kvstage[:], 0.0))
    p.op("vector", lambda e: e.memset(eps_sb[:], 1e-5))
    v_id = p.op("vector", lambda e: e.memset(kswide[:], 0.0), inc=s_vec)
    W("tensor", rf_t)
    W("tensor", v_id)
    pe_rft = p.op("tensor", lambda e: e.transpose(
        pbank[0][0:64, 0:64], rf_f, ident_f[0:64, 0:64]), inc=s_pe)
    W("vector", pe_rft)
    v_rf = p.op("vector", lambda e: e.tensor_scalar_mul(
        rfbd[0:64, 0:64], pbank[0][0:64, 0:64], 0.125), inc=s_vec)
    W("sync", v_rf)
    W("sync", t_wload)
    rf_cp = p.dma("sync", rfbd[64:128, 64:128], rfbd[0:64, 0:64], s_ld)
    W("tensor", t_b1)
    W("tensor", v_rf)
    W("tensor", rf_cp)
    for h in range(2):
        pe_b1 = p.op("tensor", lambda e, h=h: e.transpose(
            pbank[1][:, 0:8], b1stage[:, 128 * h:128 * h + 128],
            ident_f[0:8, 0:8]), inc=s_pe)
        W("vector", pe_b1)
        vb = p.op("vector", lambda e, h=h: e.tensor_copy(
            b1s[:, 8 * h:8 * h + 8], pbank[1][:, 0:8]), inc=s_vec)
        W("tensor", vb)

    stgs = [F32A[:, 4096:5120], F32A[:, 5120:6144]]
    chunks = ([("wq", j) for j in range(24)] + [("x0", j) for j in range(8)]
              + [("wo", j) for j in range(8)])
    evg = {}
    t_wq_pe = t_x0_pe = None
    for ci, (kind, j) in enumerate(chunks):
        stg = stgs[ci % 2]
        if ci >= 2:
            W("sync", evg[(ci - 2) * 8 + 7])
        src = {"wq": wqkv_in, "x0": x_in, "wo": wout_in}[kind]
        tl = p.dma("sync", stg[:], src[128 * j:128 * j + 128, :], s_ld)
        for k in range(8):
            gti = ci * 8 + k
            bi = 2 + gti % 4
            W("tensor", tl)
            if gti >= 4:
                W("tensor", evg[gti - 4])
            pe_f = p.op("tensor", lambda e, bi=bi, stg=stg, k=k: e.transpose(
                pbank[bi][:, 0:128], stg[:, 128 * k:128 * k + 128],
                ident_f[:]), inc=s_pe)
            if kind == "wq":
                dst = wqkvT[:, 3072 * k + 128 * j:3072 * k + 128 * j + 128]
            elif kind == "x0":
                dst = AR_X[:, 1024 * k + 128 * j:1024 * k + 128 * j + 128]
            else:
                dst = w_outT[:, 1024 * k + 128 * j:1024 * k + 128 * j + 128]
            W("scalar", pe_f)
            evg[gti] = p.op("scalar", lambda e, dst=dst, bi=bi: e.copy(
                dst, pbank[bi][:, 0:128]), inc=s_scl)
        if kind == "wq" and j == 23:
            t_wq_pe = p.mark(s_scl)
        if kind == "x0" and j == 7:
            t_x0_pe = p.mark(s_scl)
    t_wo_pe = p.mark(s_scl)

    mk = {}
    if stage <= 0:
        return _cut(nc, p, [s_ld, s_st, s_gp])

    NMT_RUN = NMT if stage > 0.6 else 1
    for mt in range(NMT_RUN):
        b = mt // 4
        tb = mt * MTT
        W("sync", rf_cp)
        if mt > 0:
            W("sync", t_xb[mt])
            W("sync", mk[("yevpe", mt - 1, 24 * NT2 - 1)])
            for c in range(8):
                p.dma("sync", AR_X[:, 1024 * c:1024 * (c + 1)],
                      xb_h[tb:tb + MTT, 128 * c:128 * c + 128], s_ld, transpose=True)
            t_xfm = p.mark(s_ld)
        else:
            t_xfm = t_x0_pe
        if stage <= 0.3:
            mk[("axfm", mt)] = p.mark(s_pe)
            continue

        W("tensor", t_wq_pe)
        W("tensor", t_wo_pe)
        W("tensor", t_consts)
        W("tensor", t_xfm)
        for oc in range(24):
            for nt in range(NT2):
                gi = oc * NT2 + nt
                par = pbank[gi % 2]
                key = ("yev", mt, gi - 2)
                if key in mk:
                    W("tensor", mk[key])
                elif mt > 0:
                    W("tensor", mk[("yev", mt - 1, 24 * NT2 - 2 + gi)])
                for dc in range(8):
                    pe_t = p.op("tensor", lambda e, par=par, dc=dc, oc=oc, nt=nt: e.matmul(
                        par[:],
                        wqkvT[:, 3072 * dc + 128 * oc:3072 * dc + 128 * oc + 128],
                        AR_X[:, 1024 * dc + 512 * nt:1024 * dc + 512 * nt + 512],
                        start=(dc == 0), stop=(dc == 7)),
                        inc=s_pe if dc == 7 else None)
                mk[("yevpe", mt, gi)] = pe_t
                if stage <= 0.35:
                    mk[("yev", mt, gi)] = pe_t
                    continue
                if oc < 16 or stage <= 0.4:
                    buf = q_bf if oc < 8 else k_bf
                    cc = oc % 8
                    if oc >= 16:
                        buf = q_bf
                    dst = buf[:, 1024 * cc + 512 * nt:1024 * cc + 512 * nt + 512]
                    W("scalar", pe_t)
                    ev = p.op("scalar", lambda e, dst=dst, par=par: e.copy(dst, par[:]),
                              inc=s_scl)
                else:
                    vi = ((oc - 16) * NT2 + nt) % 2
                    dst = vstage[:, 512 * vi:512 * vi + 512]
                    key2 = ("vstore", mt, (oc - 16) * NT2 + nt - 2)
                    if key2 in mk:
                        W("vector", mk[key2])
                    W("vector", pe_t)
                    ev = p.op("vector", lambda e, dst=dst, par=par: e.tensor_copy(
                        dst, par[:]), inc=s_vec)
                    W("scalar", ev)
                    st = p.dma("scalar",
                               v_h[128 * (oc - 16):128 * (oc - 16) + 128,
                                   tb + 512 * nt:tb + 512 * nt + 512], dst, s_st)
                    mk[("vstore", mt, (oc - 16) * NT2 + nt)] = st
                mk[("yev", mt, gi)] = ev
        qk_evac = p.mark(s_scl)
        if stage <= 0.7:
            mk[("axfm", mt)] = p.mark(s_pe)
            continue

        for side, buf, hbm in (("k", k_bf, kp_h), ("q", q_bf, qp_h)):
            prevside = ("q", mt - 1) if side == "k" else ("k", mt)
            for nt in range(NT2):
                W("scalar", qk_evac)
                pk = ("ssqpe", side, mt, nt - 1) if nt > 0 else \
                    ("ssqpe", prevside[0], prevside[1], NT2 - 1)
                if pk in mk:
                    W("scalar", mk[pk])
                nk = ("norm", prevside[0], prevside[1], nt)
                if nk in mk:
                    W("scalar", mk[nk])
                sq_last = None
                for c in range(8):
                    sq_last = p.op("scalar", lambda e, c=c, nt=nt, buf=buf: e.activation(
                        q2buf[:, 512 * c:512 * c + 512],
                        buf[:, 1024 * c + 512 * nt:1024 * c + 512 * nt + 512],
                        AF.Square), inc=s_scl)
                W("tensor", sq_last)
                for c in range(8):
                    pe_ss = p.op("tensor", lambda e, c=c: e.matmul(
                        pbank[4][0:16, :], Et_sb[:, 16 * c:16 * c + 16],
                        q2buf[:, 512 * c:512 * c + 512],
                        start=(c == 0), stop=(c == 7)),
                        inc=s_pe if c == 7 else None)
                mk[("ssqpe", side, mt, nt)] = pe_ss
                W("scalar", pe_ss)
                sc_n = p.op("scalar", lambda e, nt=nt: e.activation(
                    normf[:, 512 * nt:512 * nt + 512], pbank[4][0:16, :], AF.Sqrt),
                    inc=s_scl)
                W("vector", sc_n)
                pj = ("projpe", prevside[0], prevside[1])
                if pj in mk:
                    W("vector", mk[pj])
                vneg = p.op("vector", lambda e, nt=nt: e.tensor_scalar_mul(
                    normn[0:16, 512 * nt:512 * nt + 512],
                    normf[:, 512 * nt:512 * nt + 512], -1.0), inc=s_vec)
                mk[("norm", side, mt, nt)] = vneg
            for c in range(8):
                for nt in range(NT2):
                    gi = c * NT2 + nt
                    par = pbank[2 + gi % 2]
                    key = ("fexp", side, mt, gi - 2)
                    if key in mk:
                        W("tensor", mk[key])
                    elif gi < 2:
                        pk = ("fexp", "k" if side == "q" else "q",
                              mt if side == "q" else mt - 1, 8 * NT2 - 2 + gi)
                        if pk in mk:
                            W("tensor", mk[pk])
                    W("tensor", mk[("norm", side, mt, nt)])
                    p.op("tensor", lambda e, par=par, c=c, nt=nt, buf=buf: e.matmul(
                        par[:], rfbd[:],
                        buf[:, 1024 * c + 512 * nt:1024 * c + 512 * nt + 512],
                        start=True, stop=False))
                    pe_pr = p.op("tensor", lambda e, par=par, nt=nt, c=c: e.matmul(
                        par[:], E_sb[:, 128 * c:128 * c + 128],
                        normn[0:16, 512 * nt:512 * nt + 512],
                        start=False, stop=True), inc=s_pe)
                    W("scalar", pe_pr)
                    ex = p.op("scalar", lambda e, par=par, c=c, nt=nt, buf=buf: e.activation(
                        buf[:, 1024 * c + 512 * nt:1024 * c + 512 * nt + 512],
                        par[:], AF.Exp), inc=s_scl)
                    mk[("fexp", side, mt, gi)] = ex
            mk[("projpe", side, mt)] = p.mark(s_pe)
            for c in range(8):
                W("scalar", mk[("fexp", side, mt, c * NT2 + NT2 - 1)])
                p.dma("scalar", hbm[128 * c:128 * c + 128, tb:tb + MTT],
                      buf[:, 1024 * c:1024 * c + MTT], s_st)
            if side == "k":
                mk[("kst", mt)] = p.mark(s_st)
        mk[("kpv_st", mt)] = p.mark(s_st)
        if stage <= 0.9:
            mk[("axfm", mt)] = p.mark(s_pe)
            continue

        W("sync", mk[("kst", mt)])
        for tt in range(MTT // 128):
            t0 = tb + 128 * tt
            ki = tt % 2
            key = ("kvpe", mt, tt - 2)
            if key in mk:
                W("sync", mk[key])
            elif mt > 0 and tt < 2:
                W("sync", mk[("kvpe", mt - 1, MTT // 128 - 2 + tt)])
            p.dma("sync", kvload[:, 2048 * ki:2048 * ki + 1024],
                  kp_h[:, t0:t0 + 128], s_ld, transpose=True)
            l2 = p.dma("sync", kvload[:, 2048 * ki + 1024:2048 * ki + 2048],
                       v_h[:, t0:t0 + 128], s_ld, transpose=True)
            W("tensor", l2)
            first = (mt % 4 == 0) and tt == 0
            if first and mt == 4:
                W("tensor", mk[("kvev", 0)])
            for hh in range(8):
                bank = pbank[5] if hh < 4 else pbank[6]
                p.op("tensor", lambda e, bank=bank, hh=hh, ki=ki, first=first: e.matmul(
                    bank[:, 128 * (hh % 4):128 * (hh % 4) + 128],
                    kvload[:, 2048 * ki + 128 * hh:2048 * ki + 128 * hh + 128],
                    kvload[:, 2048 * ki + 1024 + 128 * hh:2048 * ki + 1024 + 128 * hh + 128],
                    start=(first and hh % 4 == 0), stop=False, skip_group_check=True))
                tok = p.op("tensor", lambda e, hh=hh, ki=ki, first=first: e.matmul(
                    pbank[7][:, hh:hh + 1],
                    kvload[:, 2048 * ki + 128 * hh:2048 * ki + 128 * hh + 128],
                    ones_b[:, 0:1],
                    start=(first and hh == 0), stop=False, skip_group_check=True),
                    inc=s_pe if hh == 7 else None)
            mk[("kvpe", mt, tt)] = tok
        if mt % 4 == 3:
            W("vector", mk[("kvpe", mt, MTT // 128 - 1)])
            for hh in range(8):
                bank = pbank[5] if hh < 4 else pbank[6]
                base = (b * 8 + hh) * 128
                p.op("vector", lambda e, bank=bank, hh=hh, base=base: e.tensor_copy(
                    kvstage[0:64, base:base + 64],
                    bank[0:64, 128 * (hh % 4):128 * (hh % 4) + 64]))
                p.op("vector", lambda e, bank=bank, hh=hh, base=base: e.tensor_copy(
                    kvstage[64:128, base + 64:base + 128],
                    bank[64:128, 128 * (hh % 4) + 64:128 * (hh % 4) + 128]))
                blk = (b * 8 + hh) * 16
                p.op("vector", lambda e, hh=hh, blk=blk: e.tensor_copy(
                    kswide[0:64, blk + 2 * hh:blk + 2 * hh + 1],
                    pbank[7][0:64, hh:hh + 1]))
                p.op("vector", lambda e, hh=hh, blk=blk: e.tensor_copy(
                    kswide[64:128, blk + 2 * hh + 1:blk + 2 * hh + 2],
                    pbank[7][64:128, hh:hh + 1]))
            mk[("kvev", b)] = p.op("vector", lambda e: e.engine_nop(), inc=s_vec)
        mk[("axfm", mt)] = p.mark(s_pe)
    if stage <= 1:
        return _cut(nc, p, [s_ld, s_st, s_gp])

    p.barrier([s_ld, s_st, s_gp, s_pe, s_vec, s_scl])
    W("tensor", mk[("kvev", 1)])
    W("tensor", mk[("yev", NMT - 1, 24 * NT2 - 1)])
    W("tensor", mk[("fexp", "k", NMT - 1, 8 * NT2 - 1)])
    W("sync", mk[("kvpe", NMT - 1, MTT // 128 - 1)])
    W("vector", mk[("kpv_st", NMT - 1)])
    W("scalar", mk[("kpv_st", NMT - 1)])
    sc_sq_last = None
    for nt in range(T // 512):
        b = nt // 8
        t0 = 512 * nt
        qi = nt % 2
        key = ("ntpe", nt - 2)
        if key in mk:
            W("sync", mk[key])
        for c in range(8):
            p.dma("sync", qtile[:, 4096 * qi + 512 * c:4096 * qi + 512 * c + 512],
                  qp_h[128 * c:128 * c + 128, t0:t0 + 512], s_ld)
        t_q = p.mark(s_ld)
        key = ("xrv", nt - 1)
        if key in mk:
            W("sync", mk[key])
        for st_ in range(2):
            p.dma("sync", xres[:, 1024 * st_:1024 * st_ + 1024],
                  x_in[t0 + 128 * st_:t0 + 128 * st_ + 128, :], s_ld)
        t_xr = p.mark(s_ld)

        W("tensor", t_q)
        W("tensor", mk[("kvev", b)])
        key = ("nhv", nt - 1)
        if key in mk:
            W("tensor", mk[key])
        for hh in range(8):
            pe_nh = p.op("tensor", lambda e, hh=hh, b=b, qi=qi: e.matmul(
                pbank[2][0:16, :], kswide[:, (b * 8 + hh) * 16:(b * 8 + hh) * 16 + 16],
                qtile[:, 4096 * qi + 512 * hh:4096 * qi + 512 * hh + 512],
                start=(hh == 0), stop=(hh == 7)),
                inc=s_pe if hh == 7 else None)
        W("vector", pe_nh)
        vv = p.op("vector", lambda e: e.tensor_copy(
            nh_sb[:], pbank[2][0:16, :]), inc=s_vec)
        mk[("nhv", nt)] = vv
        p.op("vector", lambda e: e.tensor_scalar_add(nh_sb[:], nh_sb[:], 1e-6))
        key = ("ntpe", nt - 1)
        if key in mk:
            W("vector", mk[key])
        p.vsync(s_vec)
        p.op("vector", lambda e: e.reciprocal(nh_sb[:], nh_sb[:]))
        p.vsync(s_vec)
        v_s = p.op("vector", lambda e: e.tensor_copy(s_bf[:], nh_sb[:]), inc=s_vec)
        sc_sbc = None
        sc_out = {}
        for c in range(8):
            W("tensor", v_s)
            if sc_sbc is not None:
                W("tensor", sc_sbc)
            pe_sbc = p.op("tensor", lambda e, c=c: e.matmul(
                pbank[3][:], E_sb[:, 128 * c:128 * c + 128], s_bf[:],
                start=True, stop=True), inc=s_pe)
            W("scalar", pe_sbc)
            if c >= 2:
                W("scalar", mk[("vq", nt, c - 2)])
            sc_sbc = p.op("scalar", lambda e, c=c: e.copy(
                sbc_sb[:, 512 * (c % 2):512 * (c % 2) + 512], pbank[3][:]), inc=s_scl)
            W("vector", sc_sbc)
            vq = p.op("vector", lambda e, c=c, qi=qi: e.tensor_mul(
                qtile[:, 4096 * qi + 512 * c:4096 * qi + 512 * c + 512],
                qtile[:, 4096 * qi + 512 * c:4096 * qi + 512 * c + 512],
                sbc_sb[:, 512 * (c % 2):512 * (c % 2) + 512]), inc=s_vec)
            mk[("vq", nt, c)] = vq
            W("tensor", vq)
            if c >= 2:
                W("tensor", sc_out[c - 2])
            par = pbank[c % 2]
            pe_out = p.op("tensor", lambda e, par=par, c=c, b=b, qi=qi: e.matmul(
                par[:], kvstage[:, (b * 8 + c) * 128:(b * 8 + c) * 128 + 128],
                qtile[:, 4096 * qi + 512 * c:4096 * qi + 512 * c + 512],
                start=True, stop=True), inc=s_pe)
            W("scalar", pe_out)
            key = ("ntpe", nt - 2)
            if key in mk:
                W("scalar", mk[key])
            elif nt < 2 and c == 0:
                W("scalar", mk[("kpv_st", NMT - 1)])
            sc_out[c] = p.op("scalar", lambda e, par=par, c=c, qi=qi: e.copy(
                outsc[:, 4096 * qi + 512 * c:4096 * qi + 512 * c + 512], par[:]),
                inc=s_scl)
        out_done = p.mark(s_scl)

        W("tensor", out_done)
        for st_ in range(4):
            for ot in range(2):
                par = pbank[4 + (st_ * 2 + ot) % 4]
                key = ("wev", nt, st_ * 2 + ot - 4)
                if key in mk:
                    W("tensor", mk[key])
                for c in range(8):
                    p.op("tensor", lambda e, par=par, c=c, st_=st_, ot=ot, qi=qi: e.matmul(
                        par[:],
                        outsc[:, 4096 * qi + 512 * c + 128 * st_:
                              4096 * qi + 512 * c + 128 * st_ + 128],
                        w_outT[:, 1024 * c + 512 * ot:1024 * c + 512 * ot + 512],
                        start=(c == 0), stop=False, skip_group_check=True))
                pe_w = p.op("tensor", lambda e, par=par, ot=ot: e.matmul(
                    par[:], ones_b[0:1, 0:128], bout_b[:, 512 * ot:512 * ot + 512],
                    start=False, stop=True, skip_group_check=True), inc=s_pe)
                W("vector", pe_w)
                if st_ < 2:
                    W("vector", t_xr)
                else:
                    W("vector", mk[("xrv2", nt, st_ - 2)])
                if sc_sq_last is not None and ot == 0:
                    W("vector", sc_sq_last)
                vr = p.op("vector", lambda e, par=par, st_=st_, ot=ot: e.tensor_add(
                    y1[:, 512 * ot:512 * ot + 512], par[:],
                    xres[:, 1024 * (st_ % 2) + 512 * ot:1024 * (st_ % 2) + 512 * ot + 512]),
                    inc=s_vec)
                mk[("wev", nt, st_ * 2 + ot)] = vr
            if st_ < 2:
                vr2 = mk[("wev", nt, st_ * 2 + 1)]
                W("sync", vr2)
                ld2 = p.dma("sync", xres[:, 1024 * st_:1024 * st_ + 1024],
                            x_in[t0 + 128 * (st_ + 2):t0 + 128 * (st_ + 2) + 128, :],
                            s_ld)
                mk[("xrv2", nt, st_)] = ld2
            if cy1_d is not None and nt == 0:
                vv1 = p.op("vector", lambda e: e.engine_nop(), inc=s_vec)
                W("scalar", vv1)
                p.dma("scalar", cy1_d[128 * st_:128 * st_ + 128, :], y1[:], s_st)
                p.dma("scalar", cxr_d[128 * st_:128 * st_ + 128, :],
                      xres[:, 1024 * (st_ % 2):1024 * (st_ % 2) + 1024], s_st)
            mi = st_ % 2
            p.op("vector", lambda e: e.reduce_sum(stats[:, 0:1], y1[:], axis=AXX))
            v_m = p.op("vector", lambda e: e.engine_nop(), inc=s_vec)
            W("scalar", v_m)
            sc_sq = p.op("scalar", lambda e: e.activation(
                sqjunk[:], y1[:], AF.Square, accum_out=stats[:, 1:2]), inc=s_scl)
            sc_sq_last = sc_sq
            W("vector", sc_sq)
            p.op("vector", lambda e: e.tensor_scalar(
                stats[:, 2:3], stats[:, 0:1], 1.0 / D, None, ALU.mult))
            p.op("vector", lambda e: e.tensor_scalar(
                stats[:, 3:4], stats[:, 1:2], 1.0 / D, None, ALU.mult))
            p.vsync(s_vec)
            p.op("vector", lambda e: e.tensor_mul(
                stats[:, 4:5], stats[:, 2:3], stats[:, 2:3]))
            p.vsync(s_vec)
            v_var = p.op("vector", lambda e: e.tensor_sub(
                stats[:, 5:6], stats[:, 3:4], stats[:, 4:5]), inc=s_vec)
            W("scalar", v_var)
            sc_std = p.op("scalar", lambda e: e.activation(
                stats[:, 6:7], stats[:, 5:6], AF.Sqrt, bias=eps_sb[:]), inc=s_scl)
            W("vector", sc_std)
            p.op("vector", lambda e: e.reciprocal(stats[:, 7:8], stats[:, 6:7]))
            p.vsync(s_vec)
            if cst_d is not None and nt == 0:
                vvs = p.op("vector", lambda e: e.engine_nop(), inc=s_vec)
                W("scalar", vvs)
                p.dma("scalar", cst_d[128 * st_:128 * st_ + 128, :], stats[:], s_st)
            key = ("xmst", nt, st_ - 2)
            if key in mk:
                W("vector", mk[key])
            v_xm = p.op("vector", lambda e, mi=mi: e.tensor_scalar(
                xmid_sb[:, 1024 * mi:1024 * mi + 1024], y1[:],
                stats[:, 2:3], stats[:, 7:8], ALU.subtract, ALU.mult), inc=s_vec)
            v_xmb = p.op("vector", lambda e, mi=mi: e.tensor_copy(
                xmidb_sb[:, 1024 * mi:1024 * mi + 1024],
                xmid_sb[:, 1024 * mi:1024 * mi + 1024]), inc=s_vec)
            W("scalar", v_xmb)
            p.dma("scalar", xmid32_h[t0 + 128 * st_:t0 + 128 * st_ + 128, :],
                  xmid_sb[:, 1024 * mi:1024 * mi + 1024], s_st)
            st2 = p.dma("scalar", xmidb_h[t0 + 128 * st_:t0 + 128 * st_ + 128, :],
                        xmidb_sb[:, 1024 * mi:1024 * mi + 1024], s_st)
            mk[("xmst", nt, st_)] = st2
        mk[("ntpe", nt)] = p.mark(s_pe)
        mk[("xrv", nt)] = p.mark(s_vec)
    xmid_done = p.mark(s_st)
    pe_c_done = p.mark(s_pe)
    if stage <= 2:
        return _cut(nc, p, [s_ld, s_st, s_gp])

    p.barrier([s_ld, s_st, s_gp, s_pe, s_vec, s_scl])
    W("gpsimd", pe_c_done)
    for ei in range(E):
        for dc in range(8):
            p.dma("gpsimd", W1s[:, (ei * 8 + dc) * 256:(ei * 8 + dc) * 256 + 256],
                  w1_in[ei * 1024 + 128 * dc:ei * 1024 + 128 * dc + 128, :], s_gp)
    for ei in range(E):
        for jc in range(2):
            p.dma("gpsimd", W2s[:, (ei * 2 + jc) * 1024:(ei * 2 + jc) * 1024 + 1024],
                  w2_in[ei * 256 + 128 * jc:ei * 256 + 128 * jc + 128, :], s_gp)
    t_w12 = p.mark(s_gp)

    W("vector", xmid_done)
    W("vector", pe_c_done)
    W("vector", sc_sq_last)

    sft = F32A[:, 2048:2560]
    v_lg_prev = None
    gbuf = [AR_X, AR_C]
    for mt in range(NMT):
        tb = mt * MTT
        W("sync", xmid_done)
        W("sync", pe_c_done)
        if mt > 1:
            W("sync", mk[("dxfm", mt - 2)])
        gb = gbuf[mt % 2]
        for c in range(8):
            p.dma("sync", gb[:, 1024 * c:1024 * (c + 1)],
                  xmidb_h[tb:tb + MTT, 128 * c:128 * c + 128], s_ld, transpose=True)
        t_xfm = p.mark(s_ld)
        if stage <= 2.15:
            mk[("dxfm", mt)] = p.mark(s_pe)
            continue
        W("tensor", t_xfm)
        for nt in range(NT2):
            if stage <= 2.2 and not (mt == 0 and nt == 0):
                continue
            if v_lg_prev is not None:
                W("tensor", v_lg_prev)
            for dc in range(8):
                p.op("tensor", lambda e, dc=dc, nt=nt, gb=gb: e.matmul(
                    pbank[0][0:8, :], wg_b[:, 8 * dc:8 * dc + 8],
                    gb[:, 1024 * dc + 512 * nt:1024 * dc + 512 * nt + 512],
                    start=(dc == 0), stop=False, skip_group_check=True))
            pe_g = p.op("tensor", lambda e: e.matmul(
                pbank[0][0:8, :], wgb_b[:], ones_b[0:1, :],
                start=False, stop=True, skip_group_check=True), inc=s_pe)
            W("vector", pe_g)
            v_lg = p.op("vector", lambda e: e.tensor_copy(lg_sb[:], pbank[0][0:8, :]),
                        inc=s_vec)
            v_lg_prev = v_lg
            grp = mt * NT2 + nt
            gb2 = pbank[1 + grp % 2]
            W("tensor", v_lg)
            key = ("gtr", grp - 2)
            if key in mk:
                W("tensor", mk[key])
            for st_ in range(4):
                pe_tr = p.op("tensor", lambda e, gb2=gb2, st_=st_: e.transpose(
                    gb2[:, 8 * st_:8 * st_ + 8], lg_sb[:, 128 * st_:128 * st_ + 128],
                    ident_f[0:8, 0:8]), inc=s_pe if st_ == 3 else None)
            W("scalar", pe_tr)
            key = ("gmsk", grp - 1)
            if key in mk:
                W("scalar", mk[key])
            sc_e = p.op("scalar", lambda e, gb2=gb2: e.activation(
                sft[:, 0:32], gb2[:, 0:32], AF.Exp), inc=s_scl)
            mk[("gtr", grp)] = sc_e
            W("vector", sc_e)
            for st_ in range(4):
                p.op("vector", lambda e, st_=st_: e.reduce_sum(
                    sft[:, 32 + st_:33 + st_], sft[:, 8 * st_:8 * st_ + 8], axis=AXX))
            for st_ in range(4):
                p.op("vector", lambda e, st_=st_: e.reciprocal(
                    sft[:, 36 + st_:37 + st_], sft[:, 32 + st_:33 + st_]))
            for st_ in range(4):
                p.op("vector", lambda e, st_=st_: e.tensor_scalar_mul(
                    sft[:, 40 + 8 * st_:48 + 8 * st_], sft[:, 8 * st_:8 * st_ + 8],
                    sft[:, 36 + st_:37 + st_]))
            for st_ in range(4):
                p.op("vector", lambda e, st_=st_: e.max(
                    out=sft[:, 72 + 8 * st_:80 + 8 * st_],
                    in_=sft[:, 40 + 8 * st_:48 + 8 * st_]))
            for st_ in range(4):
                p.op("vector", lambda e, st_=st_: e.tensor_scalar(
                    sft[:, 104 + 8 * st_:112 + 8 * st_],
                    sft[:, 40 + 8 * st_:48 + 8 * st_],
                    sft[:, 73 + 8 * st_:74 + 8 * st_], None, ALU.is_ge))
            v_msk = None
            for st_ in range(4):
                g = (tb + 512 * nt) // 128 + st_
                v_msk = p.op("vector", lambda e, g=g, st_=st_: e.tensor_mul(
                    masked_all[:, 8 * g:8 * g + 8],
                    sft[:, 40 + 8 * st_:48 + 8 * st_],
                    sft[:, 104 + 8 * st_:112 + 8 * st_]),
                    inc=s_vec if st_ == 3 else None)
            mk[("gmsk", grp)] = v_msk
        mk[("dxfm", mt)] = p.mark(s_pe)

    if stage <= 2.3:
        return _cut(nc, p, [s_ld, s_st, s_gp])
    for i in range(32):
        p.op("vector", lambda e, i=i: e.tensor_add(
            part_sb[:, 8 * i:8 * i + 8], masked_all[:, 8 * i:8 * i + 8],
            masked_all[:, 8 * (i + 32):8 * (i + 32) + 8]))
    v_part = p.op("vector", lambda e: e.engine_nop(), inc=s_vec)
    W("scalar", v_part)
    st_cc = p.dma("scalar", cc_in[:], part_sb[:], s_st)
    W("gpsimd", st_cc)
    cc_done = p.op("gpsimd", lambda e: e.collective_compute(
        "AllReduce", ALU.add, replica_groups=[list(range(8))],
        ins=[cc_in[:]], outs=[cc_out[:]]), inc=s_cc)
    W("gpsimd", cc_done)
    t_den = p.dma("gpsimd", den_sb[:], cc_out[:], s_gp)
    W("vector", t_den)
    p.op("vector", lambda e: e.tensor_scalar_add(den_sb[:], den_sb[:], 1e-6))
    p.vsync(s_vec)
    p.op("vector", lambda e: e.reciprocal(den_sb[:], den_sb[:]))
    p.vsync(s_vec)
    v_den = p.op("vector", lambda e: e.tensor_scalar_mul(den_sb[:], den_sb[:], 16.0),
                 inc=s_vec)
    if stage <= 2.6:
        return _cut(nc, p, [s_ld, s_st, s_gp, s_cc])

    exfm_t = {}

    def _eload(mt):
        tb = mt * MTT
        if mt > 0:
            W("sync", mk[("e1pe", mt - 1)])
        else:
            W("sync", mk[("dxfm", NMT - 1)])
        for c in range(8):
            p.dma("sync", AR_X[:, 1024 * c:1024 * (c + 1)],
                  xmidb_h[tb:tb + MTT, 128 * c:128 * c + 128], s_ld, transpose=True)
        exfm_t[mt] = p.mark(s_ld)

    def _e1(mt):
        W("tensor", exfm_t[mt])
        if mt == 0:
            W("tensor", t_w12)
            W("tensor", v_lg_prev)
            W("tensor", mk[("gtr", 14)])
            W("tensor", mk[("gtr", 15)])
        for ei in range(E):
            for half in range(2):
                g2 = ei * 2 + half
                bankset = (0, 1) if g2 % 2 == 0 else (2, 3)
                for nt in range(NT2):
                    key = ("hevg", mt, (g2 - 2) * NT2 + nt)
                    if key in mk:
                        W("tensor", mk[key])
                    elif mt > 0 and g2 == 0:
                        W("tensor", mk[("hevg", mt - 1, 14 * NT2 + nt)])
                    elif mt > 0 and g2 == 1:
                        W("tensor", mk[("sbev", mt - 1, 6 + nt)])
                for dc in range(8):
                    for nt in range(NT2):
                        pe_h = p.op("tensor", lambda e, bankset=bankset, nt=nt, dc=dc, ei=ei, half=half: e.matmul(
                            pbank[bankset[nt]][:],
                            W1s[:, (ei * 8 + dc) * 256 + 128 * half:
                                (ei * 8 + dc) * 256 + 128 * half + 128],
                            AR_X[:, 1024 * dc + 512 * nt:1024 * dc + 512 * nt + 512],
                            start=(dc == 0), stop=(dc == 7)),
                            inc=s_pe if (dc == 7 and nt == NT2 - 1) else None)
                W("scalar", pe_h)
                if mt > 0 and g2 == 0:
                    W("scalar", mk[("e2pe", mt - 1)])
                for nt in range(NT2):
                    col = g2 * 1024 + 512 * nt
                    sc_g = p.op("scalar", lambda e, bankset=bankset, nt=nt, col=col, ei=ei, half=half: e.activation(
                        hgm[:, col:col + 512], pbank[bankset[nt]][:], AF.Gelu,
                        bias=b1s[:, 8 * half + ei:8 * half + ei + 1]), inc=s_scl)
                    mk[("hevg", mt, g2 * NT2 + nt)] = sc_g
        mk[("e1pe", mt)] = p.mark(s_pe)

    _eload(0)
    _e1(0)

    W("tensor", mk[("gtr", 15)])
    W("tensor", mk[("gtr", 14)])
    rta = F32A[:, 2560:3072]
    v_rtall = None
    for i in range(32):
        p.op("vector", lambda e, i=i: e.tensor_mul(
            rta[:, 16 * i:16 * i + 8], masked_all[:, 8 * i:8 * i + 8],
            den_sb[:, 8 * i:8 * i + 8]))
        v_rtall = p.op("vector", lambda e, i=i: e.tensor_mul(
            rta[:, 16 * i + 8:16 * i + 16],
            masked_all[:, 8 * (i + 32):8 * (i + 32) + 8],
            den_sb[:, 8 * i:8 * i + 8]), inc=s_vec if i == 31 else None)
    W("tensor", v_rtall)
    v_pm_b = None
    for half in range(2):
        if half == 1:
            W("tensor", v_pm_b)
        for j in range(16):
            i = 16 * half + j
            par = pbank[4 + j // 4]
            col = 128 * (j % 4)
            pe_rt = p.op("tensor", lambda e, par=par, col=col, i=i: e.transpose(
                par[0:16, col:col + 128], rta[:, 16 * i:16 * i + 16], ident_f[:]),
                inc=s_pe if j == 15 else None)
        W("vector", pe_rt)
        for j in range(16):
            i = 16 * half + j
            par = pbank[4 + j // 4]
            col = 128 * (j % 4)
            v_pm_b = p.op("vector", lambda e, par=par, col=col, i=i: e.tensor_copy(
                route_pm[:, 128 * i:128 * i + 128], par[0:16, col:col + 128]),
                inc=s_vec if j == 15 else None)
    if stage <= 2.8:
        return _cut(nc, p, [s_ld, s_st, s_gp, s_cc])
    LO, HI, TH, CN, CD, NE, CH = range(7)
    p.op("vector", lambda e: e.memset(bis[:, LO:LO + 1], -1e-6))
    p.op("vector", lambda e: e.tensor_scalar(
        junk_bf[:], route_pm[:], 0.0, None, ALU.add, ALU.max,
        accum_out=bis[:, HI:HI + 1]))
    for it in range(BIS_ITERS):
        p.vsync(s_vec)
        p.op("vector", lambda e: e.tensor_add(
            bis[:, TH:TH + 1], bis[:, LO:LO + 1], bis[:, HI:HI + 1]))
        p.vsync(s_vec)
        p.op("vector", lambda e: e.tensor_scalar_mul(
            bis[:, TH:TH + 1], bis[:, TH:TH + 1], 0.5))
        p.vsync(s_vec)
        p.op("vector", lambda e: e.tensor_scalar(
            junk_bf[:], route_pm[:], bis[:, TH:TH + 1], None,
            ALU.is_gt, ALU.add, accum_out=bis[:, CN:CN + 1]))
        p.vsync(s_vec)
        p.op("vector", lambda e: e.tensor_scalar(
            bis[:, CD:CD + 1], bis[:, CN:CN + 1], float(KCAP), None, ALU.is_ge))
        p.vsync(s_vec)
        p.op("vector", lambda e: e.tensor_scalar(
            bis[:, 7:8], bis[:, CD:CD + 1], -1.0, 1.0, ALU.mult, ALU.add))
        p.op("vector", lambda e: e.tensor_mul(
            bis[:, 8:9], bis[:, CD:CD + 1], bis[:, TH:TH + 1]))
        p.vsync(s_vec)
        p.op("vector", lambda e: e.tensor_mul(
            bis[:, 9:10], bis[:, 7:8], bis[:, LO:LO + 1]))
        p.op("vector", lambda e: e.tensor_mul(
            bis[:, 10:11], bis[:, CD:CD + 1], bis[:, HI:HI + 1]))
        p.op("vector", lambda e: e.tensor_mul(
            bis[:, 11:12], bis[:, 7:8], bis[:, TH:TH + 1]))
        p.vsync(s_vec)
        p.op("vector", lambda e: e.tensor_add(
            bis[:, LO:LO + 1], bis[:, 8:9], bis[:, 9:10]))
        p.op("vector", lambda e: e.tensor_add(
            bis[:, HI:HI + 1], bis[:, 10:11], bis[:, 11:12]))
    p.vsync(s_vec)
    p.op("vector", lambda e: e.tensor_scalar(
        selhi[:], route_pm[:], bis[:, HI:HI + 1], None,
        ALU.is_gt, ALU.add, accum_out=bis[:, CH:CH + 1]))
    p.vsync(s_vec)
    p.op("vector", lambda e: e.tensor_scalar(
        bis[:, NE:NE + 1], bis[:, CH:CH + 1], -1.0, float(KCAP), ALU.mult, ALU.add))
    p.vsync(s_vec)
    p.op("vector", lambda e: e.tensor_scalar(
        cand_a[:], route_pm[:], bis[:, LO:LO + 1], None, ALU.is_gt))
    p.op("vector", lambda e: e.tensor_scalar(
        cand_b[:], route_pm[:], bis[:, HI:HI + 1], None, ALU.is_le))
    p.op("vector", lambda e: e.tensor_mul(cand_a[:], cand_a[:], cand_b[:]))
    p.op("vector", lambda e: e.tensor_tensor_scan(
        tier[:], cand_a[:], cand_a[:], 0.0, ALU.add, ALU.max))
    p.op("vector", lambda e: e.tensor_scalar(
        cand_b[:], tier[:], bis[:, NE:NE + 1], None, ALU.is_le))
    p.op("vector", lambda e: e.tensor_mul(cand_a[:], cand_a[:], cand_b[:]))
    v_sel = p.op("vector", lambda e: e.tensor_add(selpm_t[:], selhi[:], cand_a[:]),
                 inc=s_vec)
    W("gpsimd", v_sel)
    t_s0 = p.dma("gpsimd", sel_b0[:], selpm_t[0:8, :], s_gp)
    t_s1 = p.dma("gpsimd", sel_b1[:], selpm_t[8:16, :], s_gp)
    if route_d is not None:
        W("scalar", v_sel)
        p.dma("scalar", route_d[:], route_pm[:], s_st)
    if sel_d is not None:
        W("scalar", v_sel)
        p.dma("scalar", sel_d[0:8, :], selpm_t[0:8, :], s_st)
        p.dma("scalar", sel_d[8:16, :], selpm_t[8:16, :], s_st)

    if stage <= 3:
        return _cut(nc, p, [s_ld, s_st, s_gp, s_cc])

    out_final = None
    for mt in range(NMT):
        b = mt // 4
        tb = mt * MTT
        nb = tb - b * N
        selb = sel_b0 if b == 0 else sel_b1
        if mt > 0:
            _e1(mt)
        if mt + 1 < NMT:
            _eload(mt + 1)

        for ei in range(E):
            par = pbank[2 + ei % 2]
            key = ("sbev", mt, ei - 2)
            if key in mk:
                W("tensor", mk[key])
            if ei < 2:
                W("tensor", mk[("hevg", mt, 15 * NT2 + ei)])
            if mt > 0 and ei < 2:
                W("tensor", mk[("eev", mt - 1, MTT // 128 - 1)])
            if mt == 0 and ei == 0:
                W("tensor", t_s0)
                W("tensor", t_s1)
            pe_sb = p.op("tensor", lambda e, par=par, ei=ei, nb=nb, selb=selb: e.matmul(
                par[:], rowsel[:, 128 * ei:128 * ei + 128],
                selb[:, nb:nb + 512], start=True, stop=True), inc=s_pe)
            par2 = pbank[4 + ei % 2]
            key = ("sbev2", mt, ei - 2)
            if key in mk:
                W("tensor", mk[key])
            pe_sb2 = p.op("tensor", lambda e, par2=par2, ei=ei, nb=nb, selb=selb: e.matmul(
                par2[:], rowsel[:, 128 * ei:128 * ei + 128],
                selb[:, nb + 512:nb + 1024], start=True, stop=True), inc=s_pe)
            W("scalar", pe_sb)
            sc_a = p.op("scalar", lambda e, par=par, ei=ei: e.copy(
                selbc[:, 1024 * ei:1024 * ei + 512], par[:]), inc=s_scl)
            mk[("sbev", mt, ei)] = sc_a
            W("scalar", pe_sb2)
            sc_b = p.op("scalar", lambda e, par2=par2, ei=ei: e.copy(
                selbc[:, 1024 * ei + 512:1024 * ei + 1024], par2[:]), inc=s_scl)
            mk[("sbev2", mt, ei)] = sc_b
        selbc_done = p.mark(s_scl)

        for g2 in range(16):
            ei = g2 // 2
            for nt in range(NT2):
                col = g2 * 1024 + 512 * nt
                W("vector", mk[("hevg", mt, g2 * NT2 + nt)])
                W("vector", mk[("sbev2" if nt else "sbev", mt, ei)])
                vm = p.op("vector", lambda e, col=col, ei=ei, nt=nt: e.tensor_mul(
                    hgm[:, col:col + 512], hgm[:, col:col + 512],
                    selbc[:, 1024 * ei + 512 * nt:1024 * ei + 512 * nt + 512]),
                    inc=s_vec)
                mk[("hev", mt, g2 * NT2 + nt)] = vm
        h_done = p.mark(s_vec)
        if hg_d is not None and sbc_d is not None and mt == 0:
            W("scalar", h_done)
            p.dma("scalar", hg_d[:], hgm[:], s_st)
            p.dma("scalar", sbc_d[:], selbc[:], s_st)

        W("tensor", h_done)
        for st_ in range(MTT // 128):
            t0 = tb + 128 * st_
            n0 = nb + 128 * st_
            pidx = st_ % 2
            banks = (4, 5) if pidx == 0 else (6, 7)
            key = ("eev", mt, st_ - 2)
            if key in mk:
                W("tensor", mk[key])
            for ot in range(2):
                for kk in range(16):
                    ei, half = kk // 2, kk % 2
                    p.op("tensor", lambda e, banks=banks, ot=ot, ei=ei, half=half, st_=st_, kk=kk: e.matmul(
                        pbank[banks[ot]][:],
                        hgm[:, (ei * 2 + half) * 1024 + 128 * st_:
                            (ei * 2 + half) * 1024 + 128 * st_ + 128],
                        W2s[:, (ei * 2 + half) * 1024 + 512 * ot:
                            (ei * 2 + half) * 1024 + 512 * ot + 512],
                        start=(kk == 0), stop=False, skip_group_check=True))
                pe_e2 = p.op("tensor", lambda e, banks=banks, ot=ot, selb=selb, n0=n0: e.matmul(
                    pbank[banks[ot]][:], selb[:, n0:n0 + 128],
                    b2s[:, 512 * ot:512 * ot + 512],
                    start=False, stop=True, skip_group_check=True),
                    inc=s_pe if ot == 1 else None)
            if ps_d is not None and mt == 0:
                W("scalar", pe_e2)
                for ot in range(2):
                    ps_ev = p.op("scalar", lambda e, banks=banks, ot=ot: e.copy(
                        sqjunk[:, 512 * ot:512 * ot + 512] if False else
                        F32A[:, 3072 + 512 * ot:3072 + 512 * ot + 512],
                        pbank[banks[ot]][:]), inc=s_scl)
                W("scalar", ps_ev)
                p.dma("scalar", ps_d[128 * st_:128 * st_ + 128, :],
                      F32A[:, 3072:4096], s_st)
            key = ("hpv", mt, st_ - 2)
            if key in mk:
                W("scalar", mk[key])
            elif mt > 0 and st_ < 2:
                W("scalar", mk[("hpv", mt - 1, MTT // 128 - 2 + st_)])
            if mt == 0 and st_ < 2:
                W("scalar", v_sel)
            t_hp = p.dma("scalar", hpad[:, 1024 * pidx:1024 * pidx + 1024],
                         xmid32_h[t0:t0 + 128, :], s_st)
            W("vector", pe_e2)
            W("vector", t_hp)
            if sc_sq_last is not None:
                W("vector", sc_sq_last)
            for ot in range(2):
                p.op("vector", lambda e, banks=banks, ot=ot, pidx=pidx: e.tensor_add(
                    y1[:, 512 * ot:512 * ot + 512], pbank[banks[ot]][:],
                    hpad[:, 1024 * pidx + 512 * ot:1024 * pidx + 512 * ot + 512]))
            v_y2 = p.op("vector", lambda e: e.engine_nop(), inc=s_vec)
            if y1_d is not None and mt == 0:
                W("scalar", v_y2)
                p.dma("scalar", y1_d[128 * st_:128 * st_ + 128, :], y1[:], s_st)
            mk[("hpv", mt, st_)] = v_y2
            mk[("eev", mt, st_)] = v_y2
            p.op("vector", lambda e: e.reduce_sum(stats[:, 0:1], y1[:], axis=AXX))
            v_m = p.op("vector", lambda e: e.engine_nop(), inc=s_vec)
            W("scalar", v_m)
            if mt == 0 and st_ == 0:
                W("scalar", v_sel)
            sc_sq = p.op("scalar", lambda e: e.activation(
                sqjunk[:], y1[:], AF.Square, accum_out=stats[:, 1:2]), inc=s_scl)
            sc_sq_last = sc_sq
            W("vector", sc_sq)
            p.op("vector", lambda e: e.tensor_scalar(
                stats[:, 2:3], stats[:, 0:1], 1.0 / D, None, ALU.mult))
            p.op("vector", lambda e: e.tensor_scalar(
                stats[:, 3:4], stats[:, 1:2], 1.0 / D, None, ALU.mult))
            p.vsync(s_vec)
            p.op("vector", lambda e: e.tensor_mul(
                stats[:, 4:5], stats[:, 2:3], stats[:, 2:3]))
            p.vsync(s_vec)
            v_var = p.op("vector", lambda e: e.tensor_sub(
                stats[:, 5:6], stats[:, 3:4], stats[:, 4:5]), inc=s_vec)
            W("scalar", v_var)
            sc_std = p.op("scalar", lambda e: e.activation(
                stats[:, 6:7], stats[:, 5:6], AF.Sqrt, bias=eps_sb[:]), inc=s_scl)
            W("vector", sc_std)
            p.op("vector", lambda e: e.reciprocal(stats[:, 7:8], stats[:, 6:7]))
            p.vsync(s_vec)
            key = ("yst", mt, st_ - 2)
            if key in mk:
                W("vector", mk[key])
            if mt == 0 and st_ < 2:
                W("vector", v_sel)
            v_out = p.op("vector", lambda e, pidx=pidx: e.tensor_scalar(
                yout_sb[:, 1024 * pidx:1024 * pidx + 1024], y1[:],
                stats[:, 2:3], stats[:, 7:8], ALU.subtract, ALU.mult), inc=s_vec)
            W("sync", v_out)
            out_final = p.dma("sync", y_out[t0:t0 + 128, :],
                              yout_sb[:, 1024 * pidx:1024 * pidx + 1024], s_out)
            mk[("yst", mt, st_)] = out_final
        mk[("exfm", mt)] = p.mark(s_pe)
        mk[("e2pe", mt)] = p.mark(s_pe)

    W("sync", out_final)
    W("gpsimd", out_final)

    return _emit(nc, p)


def _emit(nc, p):
    with nc.Block() as block:
        @block.sync
        def _(e):
            for fn in p.ops["sync"]:
                fn(e)

        @block.scalar
        def _(e):
            for fn in p.ops["scalar"]:
                fn(e)

        @block.vector
        def _(e):
            for fn in p.ops["vector"]:
                fn(e)

        @block.tensor
        def _(e):
            for fn in p.ops["tensor"]:
                fn(e)

        @block.gpsimd
        def _(e):
            for fn in p.ops["gpsimd"]:
                fn(e)

    return nc


def _cut(nc, p, sems):
    for s in sems:
        p.wait("sync", p.mark(s))
    return _emit(nc, p)


def _host_inputs(inputs, core):
    x = np.ascontiguousarray(
        np.asarray(inputs["L0_attn_x"])[2 * core:2 * core + 2].reshape(T, D))
    emat = np.zeros((16, D), dtype=np.float32)
    for h in range(16):
        emat[h, 64 * h:64 * h + 64] = 1.0
    rowsel = np.zeros((E, E * 128), dtype=np.float32)
    for e in range(E):
        rowsel[e, 128 * e:128 * e + 128] = 1.0
    m = {
        "x": x,
        "w_qkv": inputs["L0_attn_w_qkv"],
        "w_out": inputs["L0_attn_w_out"],
        "b_out": np.asarray(inputs["L0_attn_b_out"]).reshape(1, D),
        "rf": inputs["L0_attn_random_features"],
        "w_gate": inputs["L0_ffn_w_gate_w"],
        "w_gate_b": np.asarray(inputs["L0_ffn_w_gate_b"]).reshape(1, E),
        "W1": np.asarray(inputs["L0_ffn_W1"]).reshape(E * D, SH),
        "b1": inputs["L0_ffn_b1"],
        "W2": np.asarray(inputs["L0_ffn_W2"]).reshape(E * SH, D),
        "b2": inputs["L0_ffn_b2"],
        "ident128": np.eye(128, dtype=np.float32),
        "onesM": np.ones((128, 512), dtype=np.float32),
        "Emat": emat,
        "Etmat": np.ascontiguousarray(emat.T),
        "Rowsel": rowsel,
    }
    return {k: np.ascontiguousarray(np.asarray(v, dtype=np.float32)) for k, v in m.items()}


def kernel(**inputs):
    nc = build_nc()
    in_maps = [_host_inputs(inputs, c) for c in range(8)]
    res = run_bass_kernel_spmd(nc, in_maps, core_ids=list(range(8)))
    out = np.concatenate([r["y"].reshape(B2, N, D) for r in res.results], axis=0)
    return np.ascontiguousarray(out.astype(np.float32))
